# version 07
# baseline (speedup 1.0000x reference)
import contextlib
import os
import numpy as np
import ml_dtypes
import concourse.bass as bass
import concourse.mybir as mybir
from concourse.bass_utils import run_bass_kernel_spmd
from concourse.bass_core_rust import IndirectOffsetOnAxis

F32 = mybir.dt.float32
BF16 = mybir.dt.bfloat16
I32 = mybir.dt.int32
AF = mybir.ActivationFunctionType
OP = mybir.AluOpType
AX = mybir.AxisListType.X
BF = ml_dtypes.bfloat16

B, S, H = 4, 2048, 2048
NH, NKV, D = 16, 4, 128
CHUNK, NCH = 256, 8
E, TOPK, II = 32, 4, 1024
CAP = 256
TCORE = 1024
NTT = 8
SCALE = float(D) ** -0.5
NEG = -1e9
CHUNKSETS = [[0, 1, 6, 7], [2, 3, 4, 5]]
PADC = [3, 4, 7, 8]
NCT = sum(PADC)
NTROW = E * (CAP + 1) + NTT * TOPK + 1
SROWS = 1 + TCORE * TOPK + NTROW + 1
EPS1 = 1e-5
EPS2 = 1e-6

_DBG = []


class SemCtr:
    __slots__ = ("h", "val")

    def __init__(self, handle):
        self.h = handle
        self.val = 0


class Arena:
    pass_marker = 1
    """Manual SBUF sub-allocator over one big bf16 tensor [128, W]."""

    def __init__(self, tensor, width):
        self.t = tensor
        self.free = [(0, width)]
        self.named = {}

    def alloc(self, name, shape, dtype, side="left"):
        els = int(np.prod(shape[1:]))
        if dtype == F32 or dtype == I32:
            els *= 2
        els = (els + 15) // 16 * 16
        blocks = enumerate(self.free)
        if side == "right":
            blocks = reversed(list(blocks))
        for i, (off, w) in blocks:
            if w >= els:
                if side == "right":
                    self.free[i] = (off, w - els)
                    start = off + w - els
                else:
                    self.free[i] = (off + els, w - els)
                    start = off
                if self.free[i][1] == 0:
                    self.free.pop(i)
                self.named[name] = (start, els, shape, dtype)
                return self.view(name)
        raise MemoryError(f"arena: no space for {name} ({els} els); free={self.free}")

    def alloc_r(self, name, shape, dtype):
        return self.alloc(name, shape, dtype, side="right")

    def view(self, name):
        off, els, shape, dtype = self.named[name]
        ap = self.t[:, off:off + els]
        if dtype in (F32, I32):
            ap = ap.bitcast(dtype)
        npart = shape[0]
        if npart != 128:
            ap = ap[0:npart]
        inner = list(shape[1:])
        if len(inner) == 0:
            inner = [1]
        tot = int(np.prod(inner))
        ap = ap[:, 0:tot]
        if len(inner) > 1:
            letters = " ".join(f"d{i}" for i in range(len(inner)))
            ap = ap.rearrange(f"p ({letters}) -> p {letters}",
                              **{f"d{i}": n for i, n in enumerate(inner)})
        return ap

    def release(self, *names):
        for name in names:
            off, els, _, _ = self.named.pop(name)
            self.free.append((off, els))
        self.free.sort()
        merged = []
        for off, w in self.free:
            if merged and merged[-1][0] + merged[-1][1] == off:
                merged[-1] = (merged[-1][0], merged[-1][1] + w)
            else:
                merged.append((off, w))
        self.free = [tuple(x) for x in merged]



def _rope_tables(positions):
    inv = 1.0 / 10000.0 ** (np.arange(0, D, 2, dtype=np.float32) / D)
    ang = positions.astype(np.float32)[None, :] * inv[:, None]
    cos, sin = np.cos(ang), np.sin(ang)
    return (np.concatenate([cos, cos], 0).astype(BF),
            np.concatenate([sin, sin], 0).astype(BF))


def _host_constants():
    c = {}
    c["ident_bf"] = np.eye(128, dtype=BF)
    c["ones_row"] = np.ones((1, 128), dtype=BF)
    c["ones_col"] = np.ones((128, 1), dtype=BF)
    c["lt_ones"] = np.ones((128, 128), dtype=BF)
    c["lt_strict"] = np.triu(np.ones((128, 128), np.float32), 1).astype(BF)
    c["e_iota"] = np.broadcast_to(np.arange(E, dtype=np.float32), (128, E)).copy()
    pay = np.zeros((128, NTT, TOPK, 2), np.int32)
    toks = np.arange(128)[:, None] + 128 * np.arange(NTT)[None, :]
    pay[:, :, :, 0] = toks[:, :, None]
    for k in range(TOPK):
        pay[:, :, k, 1] = 4 * toks + k + 1
    c["payload"] = pay
    tp = np.zeros((NTROW, 2), np.int32)
    tp[:, 1] = 1 + TCORE * TOPK + np.arange(NTROW)
    c["t_prefill"] = tp
    return c


def _core_constants(g):
    cs = CHUNKSETS[g]
    c = {}
    own_pos = np.concatenate([np.arange(cc * CHUNK, (cc + 1) * CHUNK) for cc in cs])
    c["cosq"], c["sinq"] = _rope_tables(own_pos)
    c["cosk"], c["sink"] = _rope_tables(np.arange(S))
    ct = np.zeros((NCT, 2, 128, 256), np.float32)
    idx = 0
    for s, cc in enumerate(cs):
        for kc in range(PADC[s]):
            for sub in range(2):
                if kc == cc:
                    key_in = sub * 128 + np.arange(128)
                    ct[idx + kc, sub] = np.where(
                        np.arange(256)[None, :] >= key_in[:, None], 0.0, NEG)
                elif kc > cc:
                    ct[idx + kc, sub] = NEG
        idx += PADC[s]
    c["ctile"] = ct.astype(BF)
    pm = np.zeros((8, NCH), np.float32)
    dg = np.zeros((8, NCH), np.float32)
    for st in range(8):
        qc = cs[st // 2]
        pm[st, qc:] = NEG
        dg[st, qc] = 1.0
    c["pm"] = np.broadcast_to(pm[None], (128, 8, NCH)).copy()
    c["diag"] = np.broadcast_to(dg[None], (128, 8, NCH)).copy()
    return c



def build_program(dbg=()):
    dbg = set(dbg)
    nc = bass.Bass("TRN2", target_bir_lowering=False)

    def din(name, shape, dtype):
        return nc.dram_tensor(name, list(shape), dtype, kind="ExternalInput").ap()

    x_own = din("x_own", (TCORE, H), F32)
    x_ctx = din("x_ctx", (S, H), F32)
    wq_in = din("wq", (H, NH * D), F32)
    wk_in = din("wk", (H, NKV * D), F32)
    wv_in = din("wv", (H, NKV * D), F32)
    wo_in = din("wo", (NH * D, H), F32)
    nw_in = din("norm_w", (H,), F32)
    gwT_in = din("gwT", (H, E), F32)
    w13_in = din("w13", (E, H, 2 * II), F32)
    w2_in = din("w2", (E, II, H), F32)
    sg_in = din("sgw", (H, II), F32)
    su_in = din("suw", (H, II), F32)
    sd_in = din("sdw", (II, H), F32)
    ident_in = din("ident_bf", (128, 128), BF16)
    onesrow_in = din("ones_row", (1, 128), BF16)
    onescol_in = din("ones_col", (128, 1), BF16)
    ltones_in = din("lt_ones", (128, 128), BF16)
    ltstrict_in = din("lt_strict", (128, 128), BF16)
    eiota_in = din("e_iota", (128, E), F32)
    payload_in = din("payload", (128, NTT, TOPK, 2), I32)
    tprefill_in = din("t_prefill", (NTROW, 2), I32)
    cosq_in = din("cosq", (128, TCORE), BF16)
    sinq_in = din("sinq", (128, TCORE), BF16)
    cosk_in = din("cosk", (128, S), BF16)
    sink_in = din("sink", (128, S), BF16)
    ctile_in = din("ctile", (NCT, 2, 128, 256), BF16)
    pm_in = din("pm", (128, 8, NCH), F32)
    cvec_in = din("cvec", (128, 8), F32)
    diag_in = din("diag", (128, 8, NCH), F32)

    y_out = nc.dram_tensor("y", [TCORE, H], F32, kind="ExternalOutput").ap()
    dbg_outs = {}
    for name, shape, dt in [
        ("x1", (TCORE, H), F32), ("kt", (128, NKV, S), F32),
        ("qt", (128, 16, TCORE), F32), ("v", (128, 16, NKV * D), F32),
        ("ot", (128, 16, TCORE), F32), ("bt", (128, TCORE), F32),
        ("h", (TCORE, H), F32), ("w", (128, NTT, E), F32),
        ("islot", (128, NTT, E), F32), ("routed", (TCORE, H), F32),
    ]:
        if name in dbg:
            dbg_outs[name] = nc.dram_tensor(
                "dbg_" + name, list(shape), dt, kind="ExternalOutput").ap()

    if "lnx0" in dbg:
        dbg_outs["lnx0"] = nc.dram_tensor(
            "dbg_lnx0", [128, H + 16], F32, kind="ExternalOutput").ap()
    if "ge0" in dbg:
        dbg_outs["ge0"] = nc.dram_tensor(
            "dbg_ge0", [128, 22, 1024], F32, kind="ExternalOutput").ap()
    if "p00" in dbg:
        dbg_outs["p00"] = nc.dram_tensor(
            "dbg_p00", [128, 16, 256], F32, kind="ExternalOutput").ap()

    h_dram = nc.dram_tensor("h_dram", [TCORE, H], BF16, kind="Internal").ap()
    t_dram = nc.dram_tensor("t_dram", [NTROW, 2], I32, kind="Internal").ap()
    s_dram = nc.dram_tensor("s_dram", [SROWS, H], BF16, kind="Internal").ap()

    Q = {k: [] for k in ("sync", "tensor", "vector", "scalar", "gpsimd")}

    step_ctx = {}

    def step(eng, waits, emit, incs=()):
        incs = list(incs)
        waits = list(waits)
        own = step_ctx.get(eng)
        if own is not None:
            waits.append((own, own.val))
            if not any(sc is own for sc, _ in incs):
                incs.append((own, 1))
        wl = [(s.h, (s.val if v is None else v)) for (s, v) in waits]
        pl = []
        for (s, n) in incs:
            s.val += n
            pl.append((s.h, n))

        def f(e, wl=wl, pl=pl, emit=emit):
            for hh, v in wl:
                e.wait_ge(hh, v)
            inst = emit(e)
            for hh, n in pl:
                inst.then_inc(hh, n)
        Q[eng].append(f)

    ARENA_W = int(os.environ.get('KERNEL_ARENA_W', 103 * 1024))

    with contextlib.ExitStack() as _stk:
        arena_t = _stk.enter_context(nc.sbuf_tensor("arena", [128, ARENA_W], BF16))
        s_LD, s_GP, s_PE, s_DV, s_AC, s_ST, s_SA, s_SB, s_SC, s_SD = (
            _stk.enter_context(nc.semaphore(nm))
            for nm in ("LD", "GP", "PE", "DV", "AC", "ST", "SA", "SB", "SC", "SD"))
        pb = [_stk.enter_context(nc.psum_tensor(f"pb{i}", [128, 512], F32))
              for i in range(8)]
        block = _stk.enter_context(nc.Block())
        A = Arena(arena_t, ARENA_W)
        LD, GP = SemCtr(s_LD), SemCtr(s_GP)
        PE, DV, AC, STo = SemCtr(s_PE), SemCtr(s_DV), SemCtr(s_AC), SemCtr(s_ST)
        SL = [SemCtr(s_SA), SemCtr(s_SB), SemCtr(s_SC), SemCtr(s_SD)]
        step_ctx["vector"] = DV
        step_ctx["scalar"] = AC
        bank_fill = [0] * 8
        bank_evac = [(DV, 0)] * 8

        def mm(bank_ap, lhsT, rhs, start, stop, waits=(), inc=False):
            step("tensor", list(waits),
                 lambda t, o=bank_ap, l=lhsT, r=rhs, st=start, sp=stop:
                 t.matmul(o, l, r, start=st, stop=sp),
                 incs=([(PE, 1)] if inc else []))

        def phase_barrier():
            sems = [(LD, None), (GP, None), (PE, None), (DV, None), (AC, None),
                    (STo, None)] + [(sl_, None) for sl_ in SL]
            for eng in ("sync", "tensor", "vector", "scalar", "gpsimd"):
                step(eng, list(sems), lambda e: e.nop())

        c_ident = A.alloc_r("ident", (128, 128), BF16)
        c_onesrow = A.alloc_r("onesrow", (1, 128), BF16)
        c_onescol = A.alloc_r("onescol", (128, 1), BF16)
        c_ltones = A.alloc_r("ltones", (128, 128), BF16)
        c_ltstrict = A.alloc_r("ltstrict", (128, 128), BF16)
        c_eiota = A.alloc_r("eiota", (128, E), F32)
        c_payload = A.alloc_r("payload", (128, NTT, TOPK, 2), I32)
        c_cosq = A.alloc("cosq", (128, TCORE), BF16)
        c_sinq = A.alloc("sinq", (128, TCORE), BF16)
        c_cosk = A.alloc("cosk", (128, S), BF16)
        c_sink = A.alloc("sink", (128, S), BF16)
        c_pm = A.alloc_r("pm", (128, 8, NCH), F32)
        c_diag = A.alloc_r("diag", (128, 8, NCH), F32)
        c_nw = A.alloc_r("nw", (128, H), F32)

        for dst, src in [
            (c_ident, ident_in), (c_onesrow, onesrow_in), (c_onescol, onescol_in),
            (c_ltones, ltones_in), (c_ltstrict, ltstrict_in), (c_eiota, eiota_in),
            (c_payload, payload_in), (c_cosq, cosq_in), (c_sinq, sinq_in),
            (c_cosk, cosk_in), (c_sink, sink_in), (c_pm, pm_in), (c_diag, diag_in),
        ]:
            step("sync", [], lambda e, d=dst, s=src: e.dma_start(out=d, in_=s),
                 [(LD, 16)])
        step("sync", [],
             lambda e: e.dma_start(out=c_nw,
                                   in_=nw_in.unsqueeze(0).broadcast_to([128, H])),
             [(LD, 16)])
        consts_done = LD.val

        step("gpsimd", [],
             lambda e: e.dma_start(out=t_dram, in_=tprefill_in), [(GP, 16)])
        zero_w = A.alloc("zrow", (128, 2048), BF16)
        step("vector", [], lambda e: e.memset(zero_w, 0.0), [(DV, 1)])
        zrow_done = DV.val
        NZTILE = (1 + TCORE * TOPK + 127) // 128
        for zi in range(NZTILE):
            r0 = zi * 128
            rows = min(128, 1 + TCORE * TOPK - r0)
            step("gpsimd", [(DV, zrow_done)],
                 lambda e, r0=r0, rows=rows, zv=zero_w: e.dma_start(
                     out=s_dram[r0:r0 + rows, :], in_=zv[0:rows, :]),
                 [(GP, 16)])
        prep_gp = GP.val

        sb_wk = A.alloc("wk", (128, 16, NKV * D), BF16)
        sb_wv = A.alloc("wv", (128, 16, NKV * D), BF16)
        for hb in range(16):
            step("gpsimd", [],
                 lambda e, hb=hb: e.dma_start(
                     out=sb_wk[:, hb, :], in_=wk_in[hb * 128:(hb + 1) * 128, :]),
                 [(GP, 16)])
            step("gpsimd", [],
                 lambda e, hb=hb: e.dma_start(
                     out=sb_wv[:, hb, :], in_=wv_in[hb * 128:(hb + 1) * 128, :]),
                 [(GP, 16)])
        kvw_done = GP.val

        sb_stats = A.alloc_r("stats", (128, 16), F32)

        def emit_ln1(x_ap, lnx_ap, scol):
            m = sb_stats[:, scol:scol + 1]
            v = sb_stats[:, scol + 1:scol + 2]
            t = sb_stats[:, scol + 2:scol + 3]
            step("vector", [], lambda e: e.reduce_sum(m, x_ap, axis=AX), [(DV, 1)])
            step("vector", [(DV, None)],
                 lambda e: e.tensor_scalar(m, m, 1.0 / H, None, OP.mult),
                 [(DV, 1)])
            dv_m = DV.val
            step("scalar", [],
                 lambda e: e.activation(out=lnx_ap, in_=x_ap, func=AF.Square,
                                        accum_out=v), [(AC, 1)])
            ac_sq = AC.val
            step("vector", [(AC, ac_sq), (DV, dv_m)],
                 lambda e: e.tensor_scalar(v, v, 1.0 / H, None, OP.mult), [(DV, 1)])
            step("vector", [], lambda e: e.tensor_mul(t, m, m), [(DV, 1)])
            step("vector", [], lambda e: e.tensor_sub(v, v, t), [(DV, 1)])
            step("vector", [], lambda e: e.tensor_scalar(v, v, EPS1, None, OP.add),
                 [(DV, 1)])
            dv_v = DV.val
            step("scalar", [(DV, dv_v)],
                 lambda e: e.activation(out=v, in_=v, func=AF.Sqrt), [(AC, 1)])
            ac_r = AC.val
            step("vector", [(AC, ac_r)],
                 lambda e: e.reciprocal(v, v), [(DV, 1)])
            step("vector", [],
                 lambda e: e.tensor_scalar(lnx_ap, x_ap, m, v, OP.subtract, OP.mult),
                 [(DV, 1)])
            return DV.val

        def transpose_tile(src_ap, dst_ap, bank, boff):
            pv = pb[bank].bitcast(BF16)[:, boff * 128:(boff + 1) * 128]
            ev_s, ev_v = bank_evac[bank]
            step("tensor", [(ev_s, ev_v)],
                 lambda t, s=src_ap, p=pv: t.transpose(p, s, c_ident), [(PE, 1)])
            bank_fill[bank] = PE.val
            step("vector", [(PE, bank_fill[bank])],
                 lambda e, d=dst_ap, p=pv: e.tensor_copy(d, p), [(DV, 1)])
            bank_evac[bank] = (DV, DV.val)

        def rope_from_psum(psum_ap, cos_ap, sin_ap, out_ap, w):
            t1 = A.view("ropetmp")[:, 0:w]
            t2 = A.view("ropetmp")[:, 512:512 + w]
            step("vector", [],
                 lambda e: e.tensor_mul(t1[0:64], psum_ap[0:64], cos_ap[0:64]),
                 [(DV, 1)])
            step("vector", [],
                 lambda e: e.tensor_mul(t2[0:64], psum_ap[64:128], sin_ap[0:64]),
                 [(DV, 1)])
            step("vector", [],
                 lambda e: e.tensor_sub(out_ap[0:64], t1[0:64], t2[0:64]), [(DV, 1)])
            step("vector", [],
                 lambda e: e.tensor_mul(t1[0:64], psum_ap[64:128], cos_ap[64:128]),
                 [(DV, 1)])
            step("vector", [],
                 lambda e: e.tensor_mul(t2[0:64], psum_ap[0:64], sin_ap[64:128]),
                 [(DV, 1)])
            step("vector", [],
                 lambda e: e.tensor_add(out_ap[64:128], t1[0:64], t2[0:64]),
                 [(DV, 1)])
            return DV.val

        A.alloc("ropetmp", (64, 1024), F32)
        if "lnx0" in dbg:
            A.alloc("dbglnx", (128, H + 16), F32)

        sb_x = A.alloc("xs", (128, 2, H), F32)
        sb_lnx = A.alloc("lnxs", (128, 2, H), BF16)
        sb_lnxTc = A.alloc("lnxTc", (128, 1, 16, 512), BF16)
        sb_KT = A.alloc_r("KT", (128, NKV, S), BF16)
        sb_V = A.alloc_r("V", (128, 16, NKV * D), BF16)
        x_free = [(DV, 0), (DV, 0)]
        lnx_free = [(PE, 0), (PE, 0)]
        lnxTc_free = [(PE, 0)]

        for cchunk in range(4):
            cslot = 0
            for tt4 in range(4):
                tt = cchunk * 4 + tt4
                slot = tt % 2
                step("sync", [x_free[slot], (LD, None)],
                     lambda e, tt=tt, slot=slot: e.dma_start(
                         out=sb_x[:, slot, :],
                         in_=x_ctx[tt * 128:(tt + 1) * 128, :]),
                     [(SL[slot], 16)])
                ld_x = SL[slot].val
                step("vector", [(SL[slot], ld_x)], lambda e: e.nop())
                step("scalar", [(SL[slot], ld_x), lnx_free[slot]],
                     lambda e: e.nop())
                dv_ln = emit_ln1(sb_x[:, slot, :], sb_lnx[:, slot, :], 4 * slot)
                x_free[slot] = (DV, dv_ln)
                if tt == 0 and "lnx0" in dbg_outs:
                    step("vector", [(DV, dv_ln)],
                         lambda e: e.tensor_copy(A.view("dbglnx")[:, 0:H],
                                                 sb_lnx[:, 0, :]), [(DV, 1)])
                    step("vector", [],
                         lambda e: e.tensor_copy(A.view("dbglnx")[:, H:H + 16],
                                                 sb_stats), [(DV, 1)])
                step("tensor", [(DV, dv_ln), lnxTc_free[cslot]], lambda e: e.nop())
                for hb in range(16):
                    transpose_tile(
                        sb_lnx[:, slot, hb * 128:(hb + 1) * 128],
                        sb_lnxTc[:, cslot, hb, tt4 * 128:(tt4 + 1) * 128],
                        6 + (hb // 8), hb % 8)
                lnx_free[slot] = (PE, PE.val)
            chunk_ready = DV.val
            for dkv in range(NKV):
                bank = dkv % 2
                ev_s, ev_v = bank_evac[bank]
                step("tensor", [(ev_s, ev_v), (DV, chunk_ready),
                                (GP, kvw_done)], lambda e: e.nop())
                for hb in range(16):
                    mm(pb[bank][:, :], sb_wk[:, hb, dkv * 128:(dkv + 1) * 128],
                       sb_lnxTc[:, cslot, hb, :], hb == 0, hb == 15,
                       inc=(hb == 15))
                bank_fill[bank] = PE.val
                cs0 = cchunk * 512
                step("vector", [(PE, bank_fill[bank])], lambda e: e.nop())
                dvv = rope_from_psum(pb[bank][:, :], c_cosk[:, cs0:cs0 + 512],
                                     c_sink[:, cs0:cs0 + 512],
                                     sb_KT[:, dkv, cs0:cs0 + 512], 512)
                bank_evac[bank] = (DV, dvv)
            for vt in range(4):
                bank = 2 + vt % 2
                ev_s, ev_v = bank_evac[bank]
                step("tensor", [(ev_s, ev_v)], lambda e: e.nop())
                for hb in range(16):
                    mm(pb[bank][:, :], sb_lnxTc[:, cslot, hb, vt * 128:(vt + 1) * 128],
                       sb_wv[:, hb, :], hb == 0, hb == 15, inc=(hb == 15))
                bank_fill[bank] = PE.val
                vidx = cchunk * 4 + vt
                step("vector", [(PE, bank_fill[bank])],
                     lambda e, vidx=vidx, bank=bank: e.tensor_copy(
                         sb_V[:, vidx, :], pb[bank][:, :]), [(DV, 1)])
                bank_evac[bank] = (DV, DV.val)
            lnxTc_free[cslot] = (PE, PE.val)

        sb_lnxT_own = A.alloc_r("lnxTown", (128, 16, TCORE), BF16)
        for tt in range(NTT):
            slot = tt % 2
            step("sync", [x_free[slot]],
                 lambda e, tt=tt, slot=slot: e.dma_start(
                     out=sb_x[:, slot, :], in_=x_own[tt * 128:(tt + 1) * 128, :]),
                 [(SL[slot], 16)])
            ld_x = SL[slot].val
            step("vector", [(SL[slot], ld_x)], lambda e: e.nop())
            step("scalar", [(SL[slot], ld_x), lnx_free[slot]], lambda e: e.nop())
            dv_ln = emit_ln1(sb_x[:, slot, :], sb_lnx[:, slot, :], 4 * slot)
            x_free[slot] = (DV, dv_ln)
            step("tensor", [(DV, dv_ln)], lambda e: e.nop())
            for hb in range(16):
                transpose_tile(sb_lnx[:, slot, hb * 128:(hb + 1) * 128],
                               sb_lnxT_own[:, hb, tt * 128:(tt + 1) * 128],
                               6 + (hb // 8), hb % 8)
            lnx_free[slot] = (PE, PE.val)
        A.release("xs", "lnxs", "lnxTc")
        phase_barrier()

        sb_QT = A.alloc_r("QT", (128, 16, TCORE), BF16)
        sb_wq_s = [A.alloc("wqA", (128, 16, 512), BF16),
                   A.alloc("wqB", (128, 16, 512), BF16)]
        wq_slot_free = [(PE, 0), (PE, 0)]
        wq_ld = [0, 0]
        for quarter in range(4):
            ws = quarter % 2
            step("gpsimd", [wq_slot_free[ws]], lambda e: e.nop())
            for hb in range(16):
                step("gpsimd", [],
                     lambda e, hb=hb, quarter=quarter, ws=ws: e.dma_start(
                         out=sb_wq_s[ws][:, hb, :],
                         in_=wq_in[hb * 128:(hb + 1) * 128,
                                   quarter * 512:(quarter + 1) * 512]),
                     [(GP, 16)])
            wq_ld[ws] = GP.val
            for dq4 in range(4):
                dq = quarter * 4 + dq4
                for tc in range(4):
                    bank = dq4
                    half = tc % 2
                    bank = (dq4 % 4)
                    pv = pb[bank][:, half * 256:(half + 1) * 256]
                    ev_s, ev_v = bank_evac[bank]
                    step("tensor", [(ev_s, ev_v), (GP, wq_ld[ws])],
                         lambda e: e.nop())
                    for hb in range(16):
                        mm(pv, sb_wq_s[ws][:, hb, dq4 * 128:(dq4 + 1) * 128],
                           sb_lnxT_own[:, hb, tc * 256:(tc + 1) * 256],
                           hb == 0, hb == 15, inc=(hb == 15))
                    bank_fill[bank] = PE.val
                    step("vector", [(PE, bank_fill[bank])], lambda e: e.nop())
                    dvv = rope_from_psum(
                        pv, c_cosq[:, tc * 256:(tc + 1) * 256],
                        c_sinq[:, tc * 256:(tc + 1) * 256],
                        sb_QT[:, dq, tc * 256:(tc + 1) * 256], 256)
                    bank_evac[bank] = (DV, dvv)
            wq_slot_free[ws] = (PE, PE.val)
        A.release("wqA", "wqB", "lnxTown", "wk", "wv")
        phase_barrier()

        c_ctile = A.alloc("ctile", (128, NCT * 2, 256), BF16)
        c_cvec = A.alloc_r("cvec", (128, 8), F32)
        for i in range(NCT * 2):
            step("sync", [], lambda e, i=i: e.dma_start(
                out=c_ctile[:, i, :],
                in_=ctile_in.rearrange("a b p q -> (a b) p q")[i]), [(LD, 16)])
        step("sync", [], lambda e: e.dma_start(out=c_cvec, in_=cvec_in), [(LD, 16)])
        ld_ctile = LD.val

        sb_km = A.alloc("kmean", (128, NKV, NCH), F32)
        sb_kmb = A.alloc("kmeanb", (128, NKV, NCH), BF16)
        kt_done = DV.val
        for kv in range(NKV):
            for n in range(NCH):
                step("vector", [], lambda e, kv=kv, n=n: e.reduce_sum(
                    sb_km[:, kv, n:n + 1], sb_KT[:, kv, n * 256:(n + 1) * 256],
                    axis=AX), [(DV, 1)])
        step("vector", [(DV, None)],
             lambda e: e.tensor_copy(sb_kmb, sb_km), [(DV, 1)])
        km_done = DV.val

        sb_BT = A.alloc_r("BT", (128, TCORE), BF16)
        sb_gsc = A.alloc("gsc", (128, 4, NCH), F32)
        sb_Ball = A.alloc("Ball", (128, 128), BF16)
        ball_pe = 0
        for st4 in range(8):
            gbank = 4
            ev_s, ev_v = bank_evac[gbank]
            step("tensor", [(ev_s, ev_v), (DV, km_done)], lambda e: e.nop())
            for hh in range(16):
                mm(pb[gbank][:, hh * 8:hh * 8 + 8],
                   sb_QT[:, hh, st4 * 128:(st4 + 1) * 128],
                   sb_kmb[:, hh // 4, :], True, True, inc=(hh == 15))
            bank_fill[gbank] = PE.val
            bdv0 = DV.val
            for hh in range(16):
                gp = sb_gsc[:, 0, :]
                t8 = sb_gsc[:, 1, :]
                sel = sb_gsc[:, 2, :]
                step("vector",
                     [(PE, bank_fill[gbank]), (PE, ball_pe)] if hh == 0 else [],
                     lambda e, hh=hh, gp=gp, st4=st4: e.scalar_tensor_tensor(
                         out=gp, in0=pb[4][:, hh * 8:hh * 8 + 8], scalar=1.0,
                         in1=c_pm[:, st4, :], op0=OP.mult, op1=OP.add), [(DV, 1)])
                step("vector", [], lambda e, gp=gp, t8=t8: e.max(t8, gp), [(DV, 1)])
                step("vector", [(DV, None)],
                     lambda e, gp=gp, t8=t8, sel=sel: e.tensor_scalar(
                         sel, gp, t8[:, 2:3], None, OP.is_ge), [(DV, 1)])
                step("vector", [], lambda e, sel=sel, st4=st4: e.tensor_max(
                    sel, sel, c_diag[:, st4, :]), [(DV, 1)])
                step("vector", [], lambda e, sel=sel, hh=hh: e.tensor_scalar(
                    sb_Ball[:, hh * 8:hh * 8 + 8], sel, 1e9, -1e9,
                    OP.mult, OP.add), [(DV, 1)])
            ball_done = DV.val
            bank_evac[gbank] = (DV, ball_done)
            tb = 5
            ev_s, ev_v = bank_evac[tb]
            step("tensor", [(ev_s, ev_v), (DV, ball_done)],
                 lambda t: t.transpose(pb[tb].bitcast(BF16)[:, 0:128], sb_Ball,
                                       c_ident), [(PE, 1)])
            bank_fill[tb] = PE.val
            ball_pe = PE.val
            step("vector", [(PE, bank_fill[tb])],
                 lambda e, st4=st4: e.tensor_copy(
                     sb_BT[:, st4 * 128:(st4 + 1) * 128],
                     pb[tb].bitcast(BF16)[:, 0:128]), [(DV, 1)])
            bank_evac[tb] = (DV, DV.val)
        bt_done = DV.val

        sb_OT = A.alloc_r("OT", (128, 16, TCORE), BF16)
        if "p00" in dbg_outs:
            A.alloc_r("dbgp", (128, 16, 256), F32)
        sb_B0 = A.alloc_r("B0", (1, 2, NCH, 256), BF16)
        sb_probs = A.alloc("probs", (128, 4, 256), BF16)
        sb_dn = A.alloc("dn", (1, 2, 256), BF16)
        sb_dnf = A.alloc("dnf", (1, 2, 256), F32)
        sb_otr = A.alloc("otr", (128, 2, 256), BF16)
        probs_free = [(PE, 0)] * 4
        probs_ready = [0] * 4
        SCOB = [0, 1, 6]
        sco_free = [bank_evac[b] for b in SCOB]
        sidx = 0
        b0_free = [(PE, 0), (PE, 0)]
        b0_ld = [0, 0]
        step("tensor", [(LD, ld_ctile), (DV, bt_done)], lambda e: e.nop())
        for hh in range(16):
            kvh = hh // 4
            for st4 in range(4):
                qs = st4 * 256
                par = (hh * 4 + st4) % 2
                bsl = par
                step("sync", [b0_free[bsl], (DV, bt_done)],
                     lambda e, hh=hh, bsl=bsl, qs=qs: e.dma_start(
                         out=sb_B0[:, bsl, :, :],
                         in_=sb_BT[hh * 8:(hh + 1) * 8, qs:qs + 256]),
                     [(SL[bsl], 16)])
                b0_ld[bsl] = SL[bsl].val
                step("tensor", [(SL[bsl], b0_ld[bsl])], lambda e: e.nop())
                ob = 2 + par
                db = 4 + par
                opsum = pb[ob][:, 0:256]
                dpsum = pb[db][0:1, 0:256]
                rep = pb[db][:, 256:512]
                ctoff = sum(PADC[:st4])
                nsub = PADC[st4] * 2
                step("tensor", [bank_evac[ob], bank_evac[db]], lambda e: e.nop())
                for si in range(nsub):
                    kc, sub = si // 2, si % 2
                    sl = sidx % 3
                    sidx += 1
                    spsum = pb[SCOB[sl]][:, 0:256]
                    step("tensor", [sco_free[sl]], lambda e: e.nop())
                    mm(spsum, c_ident, c_ctile[:, (ctoff + kc) * 2 + sub, :],
                       True, False)
                    mm(spsum, c_onesrow, sb_B0[:, bsl, kc, :],
                       False, False)
                    mm(spsum, sb_KT[:, kvh, (kc * 2 + sub) * 128:(kc * 2 + sub + 1) * 128],
                       sb_QT[:, hh, qs:qs + 256], False, True, inc=True)
                    s_fill = PE.val
                    step("scalar", [(PE, s_fill), probs_free[sl]],
                         lambda e, sl=sl, spsum=spsum: e.activation(
                             out=sb_probs[:, sl, :], in_=spsum, func=AF.Exp,
                             scale=SCALE), [(AC, 1)])
                    probs_ready[sl] = AC.val
                    sco_free[sl] = (AC, AC.val)
                    if "p00" in dbg_outs and hh == 0 and st4 == 1:
                        step("vector", [(AC, AC.val)],
                             lambda e, si=si, sl=sl: e.tensor_copy(
                                 A.view("dbgp")[:, si, :], sb_probs[:, sl, :]),
                             [(DV, 1)])
                        step("vector", [(AC, AC.val)],
                             lambda e, si=si, spsum=spsum: e.tensor_copy(
                                 A.view("dbgp")[:, 8 + si, :], spsum), [(DV, 1)])
                    step("tensor", [(AC, probs_ready[sl])], lambda e: e.nop())
                    mm(opsum, sb_V[:, kc * 2 + sub, kvh * 128:(kvh + 1) * 128],
                       sb_probs[:, sl, :], si == 0, si == nsub - 1)
                    mm(dpsum, c_onescol, sb_probs[:, sl, :], si == 0,
                       si == nsub - 1, inc=(si == nsub - 1))
                    probs_free[sl] = (PE, PE.val)
                d_fill = PE.val
                step("vector", [(PE, d_fill)],
                     lambda e, par=par, dpsum=dpsum: e.reciprocal(
                         sb_dnf[:, par, :], dpsum), [(DV, 1)])
                step("vector", [(DV, None)],
                     lambda e, par=par: e.tensor_copy(
                         sb_dn[:, par, :], sb_dnf[:, par, :]), [(DV, 1)])
                if "p00" in dbg_outs and hh == 0 and st4 in (1, 2):
                    step("vector", [(DV, None)],
                         lambda e, st4=st4, par=par: e.tensor_copy(
                             A.view("dbgp")[0:1, 13 + st4, :],
                             sb_dnf[:, par, :]), [(DV, 1)])
                rc_done = DV.val
                step("tensor", [(DV, rc_done)], lambda e: e.nop())
                mm(rep, c_onesrow, sb_dn[:, par, :], True, True, inc=True)
                rep_fill = PE.val
                step("vector", [(PE, rep_fill)],
                     lambda e, par=par, opsum=opsum: e.tensor_copy(
                         sb_otr[:, par, :], opsum), [(DV, 1)])
                bank_evac[ob] = (DV, DV.val)
                step("vector", [(DV, None)],
                     lambda e, hh=hh, qs=qs, par=par, rep=rep: e.tensor_mul(
                         sb_OT[:, hh, qs:qs + 256], sb_otr[:, par, :], rep),
                     [(DV, 1)])
                bank_evac[db] = (DV, DV.val)
                b0_free[bsl] = (PE, PE.val)
        ot_done = DV.val
        for _i, _b in enumerate(SCOB):
            bank_evac[_b] = sco_free[_i]
        _dbg_keep = {"qt": "QT", "kt": "KT", "v": "V", "bt": "BT"}
        _rel = ["probs", "dn", "dnf", "otr", "gsc", "Ball", "ctile", "kmean",
                "kmeanb", "cosq", "sinq", "cosk", "sink"]
        for _k, _nm in _dbg_keep.items():
            if _k not in dbg_outs:
                _rel.append(_nm)
        A.release(*_rel)
        phase_barrier()

        sb_wo_h = [A.alloc("wo_a", (128, 8, H), BF16),
                   A.alloc("wo_b", (128, 8, H), BF16)]

        def wo_slice(hh, ho):
            return sb_wo_h[hh // 8][:, hh % 8, ho * 512:(ho + 1) * 512]
        for hb in range(16):
            step("gpsimd", [], lambda e, hb=hb: e.dma_start(
                out=sb_wo_h[hb // 8][:, hb % 8, :],
                in_=wo_in[hb * 128:(hb + 1) * 128, :]), [(GP, 16)])
        wo_ld = GP.val
        sb_xe = A.alloc("xe", (128, 2, H), F32)
        sb_x1 = A.alloc_r("x1", (128, NTT, H), BF16)
        sb_junk = A.alloc("junk", (128, H), BF16)
        sb_s2 = A.alloc_r("s2", (128, NTT, 4), F32)
        xe_free = [(DV, 0), (DV, 0)]
        step("tensor", [(GP, wo_ld), (DV, ot_done)], lambda e: e.nop())
        for tt in range(NTT):
            slot = tt % 2
            step("sync", [xe_free[slot]],
                 lambda e, tt=tt, slot=slot: e.dma_start(
                     out=sb_xe[:, slot, :], in_=x_own[tt * 128:(tt + 1) * 128, :]),
                 [(SL[2 + slot], 16)])
            ld_xe = SL[2 + slot].val
            for ho in range(4):
                bank = ho % 2
                ev_s, ev_v = bank_evac[bank]
                step("tensor", [(ev_s, ev_v)], lambda e: e.nop())
                for hh in range(16):
                    mm(pb[bank][:, :], sb_OT[:, hh, tt * 128:(tt + 1) * 128],
                       wo_slice(hh, ho), hh == 0, hh == 15, inc=(hh == 15))
                bank_fill[bank] = PE.val
                step("vector", [(PE, bank_fill[bank]), (SL[2 + slot], ld_xe)],
                     lambda e, tt=tt, ho=ho, slot=slot, bank=bank:
                     e.scalar_tensor_tensor(
                         out=sb_x1[:, tt, ho * 512:(ho + 1) * 512],
                         in0=pb[bank][:, :], scalar=1.0,
                         in1=sb_xe[:, slot, ho * 512:(ho + 1) * 512],
                         op0=OP.mult, op1=OP.add), [(DV, 1)])
                bank_evac[bank] = (DV, DV.val)
            xe_free[slot] = (DV, DV.val)
            x1row_done = DV.val
            m2 = sb_s2[:, tt, 0:1]
            r1 = sb_s2[:, tt, 1:2]
            rho = sb_s2[:, tt, 2:3]
            tmp = sb_s2[:, tt, 3:4]
            step("vector", [], lambda e, tt=tt, m2=m2: e.reduce_sum(
                m2, sb_x1[:, tt, :], axis=AX), [(DV, 1)])
            step("vector", [(DV, None)],
                 lambda e, m2=m2: e.tensor_scalar(m2, m2, 1.0 / H, None, OP.mult),
                 [(DV, 1)])
            dv_m2 = DV.val
            step("scalar", [(DV, x1row_done)],
                 lambda e, tt=tt, tmp=tmp: e.activation(
                     out=sb_junk, in_=sb_x1[:, tt, :], func=AF.Square,
                     accum_out=tmp), [(AC, 1)])
            ac_ss = AC.val
            step("vector", [(AC, ac_ss), (DV, dv_m2)],
                 lambda e, tmp=tmp: e.tensor_scalar(tmp, tmp, 1.0 / H, None,
                                                    OP.mult), [(DV, 1)])
            step("vector", [], lambda e, m2=m2, r1=r1: e.tensor_mul(r1, m2, m2),
                 [(DV, 1)])
            step("vector", [], lambda e, tmp=tmp, r1=r1: e.tensor_sub(tmp, tmp, r1),
                 [(DV, 1)])
            step("vector", [], lambda e, tmp=tmp: e.tensor_scalar(
                tmp, tmp, EPS1, None, OP.add), [(DV, 1)])
            dv_var = DV.val
            step("scalar", [(DV, dv_var)],
                 lambda e, tmp=tmp, r1=r1: e.activation(out=r1, in_=tmp,
                                                        func=AF.Sqrt), [(AC, 1)])
            step("vector", [(AC, None)],
                 lambda e, r1=r1: e.reciprocal(r1, r1), [(DV, 1)])
            step("vector", [(DV, None)],
                 lambda e, tmp=tmp, rho=rho: e.reciprocal(rho, tmp), [(DV, 1)])
            step("vector", [(DV, None)],
                 lambda e, rho=rho: e.tensor_scalar(
                     rho, rho, -EPS1, 1.0 + EPS2, OP.mult, OP.add), [(DV, 1)])
            dv_w = DV.val
            step("scalar", [(DV, dv_w)],
                 lambda e, rho=rho: e.activation(out=rho, in_=rho, func=AF.Sqrt),
                 [(AC, 1)])
            step("vector", [(AC, None)],
                 lambda e, rho=rho: e.reciprocal(rho, rho), [(DV, 1)])
            step("vector", [(DV, None)],
                 lambda e, rho=rho, r1=r1: e.tensor_mul(rho, rho, r1), [(DV, 1)])
        x1_done = DV.val
        A.release("wo_a", "wo_b", "xe", "junk")
        phase_barrier()

        sb_h = A.alloc("hbuf", (128, 2, H), BF16)
        sb_htmp = A.alloc("htmp", (128, H), F32)
        sb_hT = A.alloc_r("hT", (128, 16, TCORE), BF16)
        sb_gw = A.alloc("gw", (128, 16, E), BF16)
        sb_mself = A.alloc_r("mself", (128, NTT, E), F32)
        sb_mselb = A.alloc_r("mselb", (128, NTT, E), BF16)
        sb_j = A.alloc_r("jranks", (128, NTT, E), F32)
        sb_w = A.alloc_r("wgt", (128, NTT, E), F32)
        sb_wt = A.alloc_r("wtk", (128, NTT, TOPK), F32)
        sb_gidx = A.alloc_r("gidx", (128, NTT, TOPK), I32)
        sb_r32 = A.alloc("r32", (128, 6, E), F32)
        dbgisl_v = None
        if "islot" in dbg_outs:
            dbgisl_v = A.alloc_r("dbgisl", (128, NTT, E), F32)
        sb_rz = A.alloc("rz", (128, E), F32)
        for hb in range(16):
            step("gpsimd", [], lambda e, hb=hb: e.dma_start(
                out=sb_gw[:, hb, :], in_=gwT_in[hb * 128:(hb + 1) * 128, :]),
                 [(GP, 16)])
        gw_ld = GP.val
        step("vector", [], lambda e: e.memset(sb_rz, 0.0))

        h_free = [(SL[0], 0), (SL[1], 0)]
        for tt in range(NTT):
            slot = tt % 2
            m2 = sb_s2[:, tt, 0:1]
            rho = sb_s2[:, tt, 2:3]
            step("vector", [(AC, None), h_free[slot]],
                 lambda e, tt=tt, m2=m2, rho=rho: e.tensor_scalar(
                     sb_htmp, sb_x1[:, tt, :], m2, rho, OP.subtract, OP.mult))
            step("vector", [],
                 lambda e, slot=slot: e.tensor_mul(
                     sb_h[:, slot, :], sb_htmp, c_nw))
            h_dv = DV.val
            step("sync", [(DV, h_dv)],
                 lambda e, tt=tt, slot=slot: e.dma_start(
                     out=h_dram[tt * 128:(tt + 1) * 128, :], in_=sb_h[:, slot, :]),
                 [(SL[slot], 16)])
            h_free[slot] = (SL[slot], SL[slot].val)
            if "h" in dbg_outs:
                step("gpsimd", [(DV, h_dv)],
                     lambda e, tt=tt, slot=slot: e.dma_start(
                         out=dbg_outs["h"][tt * 128:(tt + 1) * 128, :],
                         in_=sb_h[:, slot, :]), [(GP, 16)])
                h_free[slot] = (GP, GP.val)
        h_all = [SL[0].val, SL[1].val]

        for hb in range(16):
            step("sync", [(SL[0], h_all[0]), (SL[1], h_all[1])],
                 lambda e, hb=hb: e.dma_start_transpose(
                     sb_hT[:, hb, :], h_dram[:, hb * 128:(hb + 1) * 128]),
                 [(LD, 16)])
        hT_ld = LD.val

        logits = sb_r32[:, 0, :]
        t8r = sb_r32[:, 1, :]
        ntau = sb_r32[:, 5, 0:1]
        ew = sb_r32[:, 2, :]
        mk = sb_r32[:, 3, :]
        junk32 = sb_r32[:, 4, :]
        rb = 7
        for tt in range(NTT):
            ev_s, ev_v = bank_evac[rb]
            step("tensor", [(ev_s, ev_v), (LD, hT_ld), (GP, gw_ld)],
                 lambda e: e.nop())
            for hb in range(16):
                mm(pb[rb][:, 0:E], sb_hT[:, hb, tt * 128:(tt + 1) * 128],
                   sb_gw[:, hb, :], hb == 0, hb == 15, inc=(hb == 15))
            bank_fill[rb] = PE.val
            step("vector", [(PE, bank_fill[rb])],
                 lambda e: e.tensor_copy(logits, pb[rb][:, 0:E]))
            bank_evac[rb] = (DV, DV.val)
            step("vector", [], lambda e: e.max(t8r[:, 0:8], logits))
            step("vector", [], lambda e: e.tensor_scalar(
                ntau, t8r[:, 3:4], -1.0, None, OP.mult))
            step("vector", [], lambda e, tt=tt: e.tensor_scalar(
                sb_mself[:, tt, :], logits, t8r[:, 3:4], None, OP.is_ge))
            dv_msel = DV.val
            step("scalar", [(DV, dv_msel)],
                 lambda e: e.activation(out=ew, in_=logits, func=AF.Exp,
                                        bias=ntau, scale=1.0))
            ac_ew = AC.val
            step("vector", [(AC, ac_ew)],
                 lambda e, tt=tt: e.tensor_mul(ew, ew, sb_mself[:, tt, :]))
            step("vector", [], lambda e: e.reduce_sum(
                sb_r32[:, 5, 1:2], ew, axis=AX))
            step("vector", [], lambda e: e.reciprocal(
                sb_r32[:, 5, 2:3], sb_r32[:, 5, 1:2]))
            step("vector", [], lambda e, tt=tt: e.tensor_scalar(
                sb_w[:, tt, :], ew, sb_r32[:, 5, 2:3], None, OP.mult))
            step("vector", [], lambda e, tt=tt: e.tensor_tensor_scan(
                sb_j[:, tt, :], sb_mself[:, tt, :], sb_rz, 0.0, OP.add, OP.add))
            step("vector", [], lambda e, tt=tt: e.tensor_sub(
                sb_j[:, tt, :], sb_j[:, tt, :], sb_mself[:, tt, :]))
            step("vector", [], lambda e, tt=tt: e.tensor_copy(
                sb_mselb[:, tt, :], sb_mself[:, tt, :]))
        rt_done = DV.val

        sb_isl = A.alloc("isl", (128, E), F32)
        sb_kcols = A.alloc("kcols", (128, 8), F32)
        cb = 7
        tsc_gp0 = GP.val
        for tt in range(NTT):
            ev_s, ev_v = bank_evac[cb]
            step("tensor", [(ev_s, ev_v), (DV, rt_done)], lambda e: e.nop())
            for tp in range(tt + 1):
                mm(pb[cb][:, 0:E],
                   c_ltstrict if tp == tt else c_ltones,
                   sb_mselb[:, tp, :], tp == 0, tp == tt, inc=(tp == tt))
            bank_fill[cb] = PE.val
            step("vector", [(PE, bank_fill[cb])],
                 lambda e: e.tensor_copy(sb_isl, pb[cb][:, 0:E]))
            bank_evac[cb] = (DV, DV.val)
            step("vector", [], lambda e: e.tensor_scalar(
                junk32, sb_isl, float(CAP), None, OP.is_lt))
            step("vector", [], lambda e, tt=tt: e.tensor_mul(
                junk32, junk32, sb_mself[:, tt, :]))
            m2cap_dv = DV.val
            for k in range(TOPK):
                step("vector", [], lambda e, tt=tt, k=k: e.tensor_scalar(
                    mk, sb_j[:, tt, :], float(k), None, OP.is_equal))
                step("vector", [], lambda e: e.tensor_mul(mk, mk, junk32))
                sk = sb_kcols[:, 0:1]; ek = sb_kcols[:, 1:2]
                vk = sb_kcols[:, 2:3]; gf = sb_kcols[:, 3:4]
                u1 = sb_kcols[:, 4:5]
                step("vector", [], lambda e, sk=sk: e.scalar_tensor_tensor(
                    out=ew, in0=sb_isl, scalar=1.0, in1=mk,
                    op0=OP.mult, op1=OP.mult, accum_out=sk))
                step("vector", [], lambda e, ek=ek: e.scalar_tensor_tensor(
                    out=ew, in0=c_eiota, scalar=1.0, in1=mk,
                    op0=OP.mult, op1=OP.mult, accum_out=ek))
                step("vector", [], lambda e, vk=vk: e.reduce_sum(vk, mk, axis=AX))
                step("vector", [], lambda e, tt=tt, k=k: e.scalar_tensor_tensor(
                    out=ew, in0=sb_w[:, tt, :], scalar=1.0, in1=mk,
                    op0=OP.mult, op1=OP.mult, accum_out=sb_wt[:, tt, k:k + 1]))
                trash = float(E * (CAP + 1) + tt * TOPK + k)
                step("vector", [], lambda e, ek=ek, gf=gf: e.tensor_scalar(
                    gf, ek, 257.0, None, OP.mult))
                step("vector", [], lambda e, gf=gf, sk=sk: e.tensor_add(gf, gf, sk))
                step("vector", [], lambda e, gf=gf, vk=vk: e.tensor_mul(gf, gf, vk))
                step("vector", [], lambda e, vk=vk, u1=u1, trash=trash:
                     e.tensor_scalar(u1, vk, -trash, trash, OP.mult, OP.add))
                step("vector", [], lambda e, gf=gf, u1=u1: e.tensor_add(gf, gf, u1))
                step("vector", [], lambda e, tt=tt, k=k, gf=gf: e.tensor_copy(
                    sb_gidx[:, tt, k:k + 1], gf))
            if "islot" in dbg_outs:
                step("vector", [], lambda e, tt=tt, dv_=dbgisl_v: e.tensor_copy(
                    dv_[:, tt, :], sb_isl))
        slots_dv = DV.val
        for tt in range(NTT):
            for k in range(TOPK):
                step("gpsimd", [(DV, slots_dv), (GP, tsc_gp0)],
                     lambda e, tt=tt, k=k: e.indirect_dma_start(
                         t_dram,
                         IndirectOffsetOnAxis(ap=sb_gidx[:, tt, k:k + 1], axis=0),
                         c_payload[:, tt, k, :], None), [(GP, 16)])
        tsc_done = GP.val

        _rel2 = ["hbuf", "htmp", "gw", "r32", "rz", "isl", "kcols",
                 "B0", "stats", "ropetmp", "zrow", "pm", "diag", "payload",
                 "eiota", "ltones", "ltstrict", "onesrow", "onescol",
                 "mself", "mselb", "jranks", "cvec"]
        if "ot" not in dbg_outs:
            _rel2.append("OT")
        if "w" not in dbg_outs:
            _rel2.append("wgt")
        A.release(*_rel2)
        phase_barrier()
        sb_w13 = A.alloc("w13s", (128, 2, 2, H), BF16)
        sb_w2 = A.alloc("w2s", (128, 4, 512), BF16)
        sb_btok = A.alloc("btok", (128, 2, 2, H), BF16)
        sb_bufT = A.alloc("bufT", (128, 2, 16, 256), BF16)
        sb_act = A.alloc("acts", (128, 2, 8, 256), BF16)
        sb_silu = A.alloc("silu", (128, 256), BF16)
        sb_y = A.alloc("ybuf", (128, 2, 2, H), BF16)
        sb_tg = A.alloc("tgq", (128, 2, 2, 2), I32)
        dbg_ge0_v = None
        if "ge0" in dbg_outs:
            dbg_ge0_v = A.alloc_r("dbgge0", (128, 22, 1024), BF16)
        es_free = [(GP, 0), (GP, 0)]
        tg_ld = [0, 0]
        w13_free = [(PE, 0), (PE, 0)]
        w13_ld = [0, 0]
        w2_free = [(PE, 0)] * 4
        w2_ld = [0] * 4
        for ee in range(E):
            es = ee % 2
            step("sync", [es_free[es], (GP, tsc_done)],
                 lambda e, ee=ee, es=es: e.dma_start(
                     out=sb_tg[:, es, 0, :],
                     in_=t_dram[ee * 257:ee * 257 + 128, :]), [(SL[2], 16)])
            step("sync", [],
                 lambda e, ee=ee, es=es: e.dma_start(
                     out=sb_tg[:, es, 1, :],
                     in_=t_dram[ee * 257 + 128:ee * 257 + 256, :]), [(SL[2], 16)])
            tg_ld[es] = SL[2].val
            for half in range(2):
                step("gpsimd", [(SL[2], tg_ld[es])],
                     lambda e, es=es, half=half: e.indirect_dma_start(
                         sb_btok[:, es, half, :], None,
                         h_dram,
                         IndirectOffsetOnAxis(ap=sb_tg[:, es, half, 0:1], axis=0)),
                     [(GP, 16)])
            g_ld = GP.val
            tpb_ = 7
            for half in range(2):
                for hb in range(16):
                    pv = pb[tpb_].bitcast(BF16)[:, (hb % 8) * 128:(hb % 8) * 128 + 128]
                    ev_s, ev_v = bank_evac[tpb_]
                    step("tensor", [(ev_s, ev_v), (GP, g_ld)],
                         lambda t, es=es, half=half, hb=hb, pv=pv: t.transpose(
                             pv, sb_btok[:, es, half, hb * 128:(hb + 1) * 128],
                             c_ident), [(PE, 1)])
                    bank_fill[tpb_] = PE.val
                    step("vector", [(PE, bank_fill[tpb_])],
                         lambda e, es=es, half=half, hb=hb, pv=pv: e.tensor_copy(
                             sb_bufT[:, es, hb, half * 128:(half + 1) * 128], pv))
                    bank_evac[tpb_] = (DV, DV.val)
            bufT_dv = DV.val
            for b_ in range(8):
                ev_s, ev_v = bank_evac[b_]
                step("tensor", [(ev_s, ev_v), (DV, bufT_dv)], lambda e: e.nop())
            for hb2 in range(8):
                ws2 = hb2 % 2
                step("gpsimd", [w13_free[ws2]],
                     lambda e, ee=ee, hb2=hb2, ws2=ws2: e.dma_start(
                         out=sb_w13[:, ws2, :, :],
                         in_=w13_in[ee, hb2 * 256:(hb2 + 1) * 256, :].rearrange(
                             "(a p) f -> p a f", a=2)), [(GP, 16)])
                w13_ld[ws2] = GP.val
                step("tensor", [(GP, w13_ld[ws2])], lambda e: e.nop())
                for hb in (hb2 * 2, hb2 * 2 + 1):
                    for p in range(8):
                        mm(pb[p][:, 0:256],
                           sb_w13[:, ws2, hb % 2, p * 128:(p + 1) * 128],
                           sb_bufT[:, es, hb, :], hb == 0, False)
                        mm(pb[p][:, 256:512],
                           sb_w13[:, ws2, hb % 2, (p + 8) * 128:(p + 9) * 128],
                           sb_bufT[:, es, hb, :], False, hb == 15,
                           inc=(p == 7))
                w13_free[ws2] = (PE, PE.val)
            gu_fill = PE.val
            for p in range(8):
                bank_fill[p] = gu_fill
                step("scalar", [(PE, gu_fill), (DV, None)],
                     lambda e, p=p: e.activation(
                         out=sb_silu, in_=pb[p][:, 0:256], func=AF.Silu))
                ac_si = AC.val
                step("vector", [(AC, ac_si)],
                     lambda e, es=es, p=p: e.tensor_mul(
                         sb_act[:, es, p, :], sb_silu, pb[p][:, 256:512]))
                bank_evac[p] = (DV, DV.val)
            act_dv = DV.val
            yb = [6, 7]
            for ho in range(4):
                for ch in range(2):
                    ev_s, ev_v = bank_evac[yb[ch]]
                    step("tensor", [(ev_s, ev_v)], lambda e: e.nop())
                for i8 in range(8):
                    w2s = (ho * 8 + i8) % 4
                    step("gpsimd", [w2_free[w2s]],
                         lambda e, ee=ee, i8=i8, ho=ho, w2s=w2s: e.dma_start(
                             out=sb_w2[:, w2s, :],
                             in_=w2_in[ee, i8 * 128:(i8 + 1) * 128,
                                       ho * 512:(ho + 1) * 512]), [(GP, 16)])
                    w2_ld[w2s] = GP.val
                    step("tensor", [(GP, w2_ld[w2s]), (DV, act_dv)],
                         lambda e: e.nop())
                    for ch in range(2):
                        mm(pb[yb[ch]][:, :],
                           sb_act[:, es, i8, ch * 128:(ch + 1) * 128],
                           sb_w2[:, w2s, :], i8 == 0, i8 == 7,
                           inc=(ch == 1))
                    w2_free[w2s] = (PE, PE.val)
                bank_fill[yb[0]] = PE.val
                bank_fill[yb[1]] = PE.val
                for ch in range(2):
                    step("vector", [(PE, bank_fill[yb[ch]])],
                         lambda e, es=es, ch=ch, ho=ho: e.tensor_copy(
                             sb_y[:, es, ch, ho * 512:(ho + 1) * 512],
                             pb[yb[ch]][:, :]))
                    bank_evac[yb[ch]] = (DV, DV.val)
            y_dv = DV.val
            if "ge0" in dbg_outs and ee == 0:
                for hb in range(16):
                    step("vector", [], lambda e, hb=hb: e.tensor_copy(
                        dbg_ge0_v[:, hb, 0:256], sb_bufT[:, 0, hb, :]))
                for p8 in range(8):
                    step("vector", [], lambda e, p8=p8: e.tensor_copy(
                        dbg_ge0_v[:, 16 + p8 // 4, p8 % 4 * 256:(p8 % 4 + 1) * 256],
                        sb_act[:, 0, p8, :]))
                for ch in range(2):
                    step("vector", [], lambda e, ch=ch: e.tensor_copy(
                        dbg_ge0_v[:, 18 + ch * 2:20 + ch * 2, :].rearrange(
                            "p a b -> p (a b)"), sb_y[:, 0, ch, :]))
            for ch in range(2):
                step("gpsimd", [(DV, y_dv)],
                     lambda e, es=es, ch=ch: e.indirect_dma_start(
                         s_dram,
                         IndirectOffsetOnAxis(ap=sb_tg[:, es, ch, 1:2], axis=0),
                         sb_y[:, es, ch, :], None), [(GP, 16)])
            es_free[es] = (GP, GP.val)
        experts_done_gp = GP.val

        A.release("w13s", "w2s", "btok", "bufT", "acts", "silu", "ybuf", "tgq")
        phase_barrier()
        sb_sg = A.alloc("sgs", (128, 2, 16, 128), BF16)
        sb_su = A.alloc("sus", (128, 2, 16, 128), BF16)
        sb_ash = A.alloc("ash", (128, 8, TCORE), BF16)
        sb_sil2 = A.alloc("sil2", (128, 512), BF16)
        sgu_free = [(PE, 0), (PE, 0)]
        sgu_ld = [0, 0]
        for i8 in range(8):
            ws = i8 % 2
            step("gpsimd", [sgu_free[ws]], lambda e: e.nop())
            for hb in range(16):
                step("gpsimd", [],
                     lambda e, i8=i8, hb=hb, ws=ws: e.dma_start(
                         out=sb_sg[:, ws, hb, :],
                         in_=sg_in[hb * 128:(hb + 1) * 128,
                                   i8 * 128:(i8 + 1) * 128]), [(GP, 16)])
                step("gpsimd", [],
                     lambda e, i8=i8, hb=hb, ws=ws: e.dma_start(
                         out=sb_su[:, ws, hb, :],
                         in_=su_in[hb * 128:(hb + 1) * 128,
                                   i8 * 128:(i8 + 1) * 128]), [(GP, 16)])
            sgu_ld[ws] = GP.val
            for tc in range(2):
                gb, ub = tc, 2 + tc
                for b_ in (gb, ub):
                    ev_s, ev_v = bank_evac[b_]
                    step("tensor", [(ev_s, ev_v), (GP, sgu_ld[ws])],
                         lambda e: e.nop())
                for hb in range(16):
                    mm(pb[gb][:, :], sb_sg[:, ws, hb, :],
                       sb_hT[:, hb, tc * 512:(tc + 1) * 512],
                       hb == 0, hb == 15)
                    mm(pb[ub][:, :], sb_su[:, ws, hb, :],
                       sb_hT[:, hb, tc * 512:(tc + 1) * 512],
                       hb == 0, hb == 15, inc=(hb == 15))
                bank_fill[gb] = PE.val
                bank_fill[ub] = PE.val
                step("scalar", [(PE, bank_fill[ub]), (DV, None)],
                     lambda e, gb=gb: e.activation(
                         out=sb_sil2, in_=pb[gb][:, :], func=AF.Silu))
                ac_s2 = AC.val
                step("vector", [(AC, ac_s2)],
                     lambda e, i8=i8, tc=tc, ub=ub: e.tensor_mul(
                         sb_ash[:, i8, tc * 512:(tc + 1) * 512],
                         sb_sil2, pb[ub][:, :]))
                bank_evac[gb] = (DV, DV.val)
                bank_evac[ub] = (DV, DV.val)
            sgu_free[ws] = (PE, PE.val)
        ash_dv = DV.val
        A.release("sgs", "sus", "sil2", "hT")
        phase_barrier()

        sb_sd = A.alloc("sds", (128, 8, H), BF16)
        sb_srb = A.alloc("srb", (128, 2, 4, H), BF16)
        sb_base = A.alloc("base", (128, 2, H), F32)
        NYO = 1 if "ge0" in dbg_outs else 2
        sb_yout = A.alloc("yo", (128, NYO, H), F32)
        sb_nmr = A.alloc("nmr", (128, 2), F32)
        for i8 in range(8):
            step("gpsimd", [], lambda e, i8=i8: e.dma_start(
                out=sb_sd[:, i8, :], in_=sd_in[i8 * 128:(i8 + 1) * 128, :]),
                 [(GP, 16)])
        sd_ld = GP.val
        srb_free = [(DV, 0), (DV, 0)]
        yout_free = [(STo, 0), (STo, 0)]
        yosl = (lambda slot: slot % NYO)
        for tt in range(NTT):
            slot = tt % 2
            step("sync", [srb_free[slot], (GP, experts_done_gp)],
                 lambda e, tt=tt, slot=slot: e.dma_start(
                     out=sb_srb[:, slot, :, :],
                     in_=s_dram[1 + tt * 512:1 + (tt + 1) * 512, :].rearrange(
                         "(p a) h -> p a h", a=4)), [(SL[3], 16)])
            srb_ld = SL[3].val
            m2 = sb_s2[:, tt, 0:1]
            r1 = sb_s2[:, tt, 1:2]
            nmr = sb_nmr[:, slot:slot + 1]
            step("vector", [], lambda e, m2=m2, r1=r1, nmr=nmr: e.tensor_mul(
                nmr, m2, r1))
            step("vector", [], lambda e, nmr=nmr: e.tensor_scalar(
                nmr, nmr, -1.0, None, OP.mult))
            nmr_dv = DV.val
            step("scalar", [(DV, nmr_dv)],
                 lambda e, tt=tt, slot=slot, r1=r1, nmr=nmr: e.activation(
                     out=sb_base[:, slot, :], in_=sb_x1[:, tt, :],
                     func=AF.Identity, scale=r1, bias=nmr))
            ac_b = AC.val
            step("vector", [(AC, ac_b)],
                 lambda e, tt=tt, slot=slot: e.tensor_add(
                     sb_base[:, slot, :], sb_base[:, slot, :], sb_x1[:, tt, :]))
            step("vector", [(SL[3], srb_ld)], lambda e: e.nop())
            for k in range(TOPK):
                step("vector", [],
                     lambda e, tt=tt, slot=slot, k=k: e.scalar_tensor_tensor(
                         out=sb_base[:, slot, :], in0=sb_srb[:, slot, k, :],
                         scalar=sb_wt[:, tt, k:k + 1], in1=sb_base[:, slot, :],
                         op0=OP.mult, op1=OP.add))
            base_dv = DV.val
            srb_free[slot] = (DV, base_dv)
            for ho in range(4):
                bank = 4 + ho % 2
                ev_s, ev_v = bank_evac[bank]
                step("tensor", [(ev_s, ev_v), (GP, sd_ld), (DV, ash_dv)],
                     lambda e: e.nop())
                for i8 in range(8):
                    mm(pb[bank][:, :], sb_ash[:, i8, tt * 128:(tt + 1) * 128],
                       sb_sd[:, i8, ho * 512:(ho + 1) * 512],
                       i8 == 0, i8 == 7, inc=(i8 == 7))
                bank_fill[bank] = PE.val
                step("vector", [(PE, bank_fill[bank]), (DV, base_dv),
                                yout_free[yosl(slot)]],
                     lambda e, tt=tt, slot=slot, ho=ho, bank=bank:
                     e.scalar_tensor_tensor(
                         out=sb_yout[:, yosl(slot), ho * 512:(ho + 1) * 512],
                         in0=pb[bank][:, :], scalar=1.0,
                         in1=sb_base[:, slot, ho * 512:(ho + 1) * 512],
                         op0=OP.mult, op1=OP.add))
                bank_evac[bank] = (DV, DV.val)
            yo_dv = DV.val
            step("sync", [(DV, yo_dv)],
                 lambda e, tt=tt, slot=slot: e.dma_start(
                     out=y_out[tt * 128:(tt + 1) * 128, :],
                     in_=sb_yout[:, yosl(slot), :]), [(STo, 16)])
            yout_free[yosl(slot)] = (STo, STo.val)

        def dump(name, src_ap):
            if name not in dbg_outs:
                return
            a = src_ap.shape[1] if len(src_ap.shape) > 2 else 1
            b = src_ap.shape[-1]
            tmp = A.alloc("dbgtmp", (src_ap.shape[0], b), F32)
            for i in range(a):
                sv = src_ap[:, i, :] if a > 1 else src_ap
                dv_ = dbg_outs[name][:, i, :] if a > 1 else dbg_outs[name]
                step("vector", [(DV, None), (PE, None), (AC, None), (STo, None)],
                     lambda e, sv=sv: e.tensor_copy(tmp, sv), [(DV, 1)])
                step("sync", [(DV, None)],
                     lambda e, dv_=dv_: e.dma_start(out=dv_, in_=tmp), [(STo, 16)])
            step("vector", [(STo, None)], lambda e: e.nop())
            A.release("dbgtmp")

        if "lnx0" in dbg_outs:
            step("sync", [(DV, None)],
                 lambda e: e.dma_start(out=dbg_outs["lnx0"],
                                       in_=A.view("dbglnx")), [(STo, 16)])
        if "p00" in dbg_outs:
            for i in range(16):
                step("sync", [(DV, None), (AC, None), (PE, None)],
                     lambda e, i=i: e.dma_start(out=dbg_outs["p00"][:, i, :],
                                                in_=A.view("dbgp")[:, i, :]),
                     [(STo, 16)])

        dump("kt", sb_KT)
        dump("qt", sb_QT)
        dump("v", sb_V)
        dump("ot", sb_OT)
        dump("bt", sb_BT)
        dump("w", sb_w)
        if "ge0" in dbg_outs:
            for i in range(22):
                step("gpsimd", [(DV, None), (PE, None)],
                     lambda e, i=i: e.dma_start(out=dbg_outs["ge0"][:, i, :],
                                                in_=dbg_ge0_v[:, i, :]),
                     [(GP, 16)])
        if "islot" in dbg_outs:
            for i in range(NTT):
                step("sync", [(DV, None), (PE, None)],
                     lambda e, i=i, dv_=dbgisl_v: e.dma_start(
                         out=dbg_outs["islot"][:, i, :],
                         in_=dv_[:, i, :]), [(STo, 16)])
        if "x1" in dbg_outs:
            tmp = A.alloc("dbgtmp", (128, H), F32)
            for tt in range(NTT):
                step("vector", [(DV, None), (PE, None), (AC, None), (STo, None)],
                     lambda e, tt=tt: e.tensor_copy(tmp, sb_x1[:, tt, :]),
                     [(DV, 1)])
                step("sync", [(DV, None)],
                     lambda e, tt=tt: e.dma_start(
                         out=dbg_outs["x1"][tt * 128:(tt + 1) * 128, :], in_=tmp),
                     [(STo, 16)])
            step("vector", [(STo, None)], lambda e: e.nop())
            A.release("dbgtmp")

        final_st = STo.val
        step("sync", [(STo, final_st), (GP, None), (LD, None), (DV, None),
                      (PE, None), (AC, None)], lambda e: e.nop())

        @block.sync
        def _(sync):
            for f in Q["sync"]:
                f(sync)

        @block.tensor
        def _(tensor):
            for f in Q["tensor"]:
                f(tensor)

        @block.vector
        def _(vector):
            for f in Q["vector"]:
                f(vector)

        @block.scalar
        def _(scalar):
            for f in Q["scalar"]:
                f(scalar)

        @block.gpsimd
        def _(gpsimd):
            for f in Q["gpsimd"]:
                f(gpsimd)

    return nc


def shard_inputs(inputs):
    x = np.ascontiguousarray(inputs["L0_attn_hidden_states"])
    hc = _host_constants()
    per_core = []
    shared = dict(
        wq=inputs["L0_attn_wq"], wk=inputs["L0_attn_wk"],
        wv=inputs["L0_attn_wv"], wo=inputs["L0_attn_wo"],
        norm_w=inputs["L0_ffn_norm_w"],
        gwT=np.ascontiguousarray(inputs["L0_ffn_gate_w"].T),
        w13=inputs["L0_ffn_w13"], w2=inputs["L0_ffn_w2"],
        sgw=inputs["L0_ffn_shared_gate_w"], suw=inputs["L0_ffn_shared_up_w"],
        sdw=inputs["L0_ffn_shared_down_w"],
        ident_bf=hc["ident_bf"], ones_row=hc["ones_row"], ones_col=hc["ones_col"],
        lt_ones=hc["lt_ones"], lt_strict=hc["lt_strict"], e_iota=hc["e_iota"],
        payload=hc["payload"], t_prefill=hc["t_prefill"],
        cvec=np.broadcast_to(np.array(
            [1.0 / H, -1.0, EPS1, -EPS1, 1.0 + EPS2, 0.0, 0.0, 0.0],
            np.float32), (128, 8)).copy(),
    )
    shared = {k: np.ascontiguousarray(v) for k, v in shared.items()}
    for core in range(8):
        b, g = core // 2, core % 2
        cs = CHUNKSETS[g]
        cc = _core_constants(g)
        rows = np.concatenate([np.arange(c * CHUNK, (c + 1) * CHUNK) for c in cs])
        m = dict(shared)
        m["x_own"] = np.ascontiguousarray(x[b][rows])
        m["x_ctx"] = np.ascontiguousarray(x[b])
        for k in ("cosq", "sinq", "cosk", "sink", "ctile", "pm", "diag"):
            m[k] = np.ascontiguousarray(cc[k])
        per_core.append(m)
    return per_core


def unshard_output(results):
    out = np.zeros((B, S, H), np.float32)
    for core in range(8):
        b, g = core // 2, core % 2
        cs = CHUNKSETS[g]
        y = results[core]["y"]
        for i, c in enumerate(cs):
            out[b, c * CHUNK:(c + 1) * CHUNK, :] = y[i * CHUNK:(i + 1) * CHUNK]
    return out


def kernel(**inputs):
    nc = build_program(dbg=_DBG)
    in_maps = shard_inputs(inputs)
    res = run_bass_kernel_spmd(nc, in_maps, core_ids=list(range(8)))
    return unshard_output(res.results)


# version 08 (final)
# speedup vs baseline: 1.0694x; 1.0694x over previous
import contextlib
import os
import numpy as np
import ml_dtypes
import concourse.bass as bass
import concourse.mybir as mybir
from concourse.bass_utils import run_bass_kernel_spmd
from concourse.bass_core_rust import IndirectOffsetOnAxis

F32 = mybir.dt.float32
BF16 = mybir.dt.bfloat16
I32 = mybir.dt.int32
AF = mybir.ActivationFunctionType
OP = mybir.AluOpType
AX = mybir.AxisListType.X
BF = ml_dtypes.bfloat16

B, S, H = 4, 2048, 2048
NH, NKV, D = 16, 4, 128
CHUNK, NCH = 256, 8
E, TOPK, II = 32, 4, 1024
CAP = 256
TCORE = 1024
NTT = 8
SCALE = float(D) ** -0.5
NEG = -1e9
CHUNKSETS = [[0, 1, 6, 7], [2, 3, 4, 5]]
PADC = [3, 4, 7, 8]
NCT = sum(PADC)
NTROW = E * (CAP + 1) + NTT * TOPK + 1
SROWS = 1 + TCORE * TOPK + NTROW + 1
EPS1 = 1e-5
EPS2 = 1e-6

_DBG = []


class SemCtr:
    __slots__ = ("h", "val")

    def __init__(self, handle):
        self.h = handle
        self.val = 0


class Arena:
    pass_marker = 1
    """Manual SBUF sub-allocator over one big bf16 tensor [128, W]."""

    def __init__(self, tensor, width):
        self.t = tensor
        self.free = [(0, width)]
        self.named = {}

    def alloc(self, name, shape, dtype, side="left"):
        els = int(np.prod(shape[1:]))
        if dtype == F32 or dtype == I32:
            els *= 2
        els = (els + 15) // 16 * 16
        blocks = enumerate(self.free)
        if side == "right":
            blocks = reversed(list(blocks))
        for i, (off, w) in blocks:
            if w >= els:
                if side == "right":
                    self.free[i] = (off, w - els)
                    start = off + w - els
                else:
                    self.free[i] = (off + els, w - els)
                    start = off
                if self.free[i][1] == 0:
                    self.free.pop(i)
                self.named[name] = (start, els, shape, dtype)
                return self.view(name)
        raise MemoryError(f"arena: no space for {name} ({els} els); free={self.free}")

    def alloc_r(self, name, shape, dtype):
        return self.alloc(name, shape, dtype, side="right")

    def view(self, name):
        off, els, shape, dtype = self.named[name]
        ap = self.t[:, off:off + els]
        if dtype in (F32, I32):
            ap = ap.bitcast(dtype)
        npart = shape[0]
        if npart != 128:
            ap = ap[0:npart]
        inner = list(shape[1:])
        if len(inner) == 0:
            inner = [1]
        tot = int(np.prod(inner))
        ap = ap[:, 0:tot]
        if len(inner) > 1:
            letters = " ".join(f"d{i}" for i in range(len(inner)))
            ap = ap.rearrange(f"p ({letters}) -> p {letters}",
                              **{f"d{i}": n for i, n in enumerate(inner)})
        return ap

    def release(self, *names):
        for name in names:
            off, els, _, _ = self.named.pop(name)
            self.free.append((off, els))
        self.free.sort()
        merged = []
        for off, w in self.free:
            if merged and merged[-1][0] + merged[-1][1] == off:
                merged[-1] = (merged[-1][0], merged[-1][1] + w)
            else:
                merged.append((off, w))
        self.free = [tuple(x) for x in merged]



def _rope_tables(positions):
    inv = 1.0 / 10000.0 ** (np.arange(0, D, 2, dtype=np.float32) / D)
    ang = positions.astype(np.float32)[None, :] * inv[:, None]
    cos, sin = np.cos(ang), np.sin(ang)
    return (np.concatenate([cos, cos], 0).astype(BF),
            np.concatenate([sin, sin], 0).astype(BF))


def _host_constants():
    c = {}
    c["ident_bf"] = np.eye(128, dtype=BF)
    c["ones_row"] = np.ones((1, 128), dtype=BF)
    c["ones_col"] = np.ones((128, 1), dtype=BF)
    c["lt_ones"] = np.ones((128, 128), dtype=BF)
    c["lt_strict"] = np.triu(np.ones((128, 128), np.float32), 1).astype(BF)
    c["e_iota"] = np.broadcast_to(np.arange(E, dtype=np.float32), (128, E)).copy()
    pay = np.zeros((128, NTT, TOPK, 2), np.int32)
    toks = np.arange(128)[:, None] + 128 * np.arange(NTT)[None, :]
    pay[:, :, :, 0] = toks[:, :, None]
    for k in range(TOPK):
        pay[:, :, k, 1] = 4 * toks + k + 1
    c["payload"] = pay
    tp = np.zeros((NTROW, 2), np.int32)
    tp[:, 1] = 1 + TCORE * TOPK + np.arange(NTROW)
    c["t_prefill"] = tp
    return c


def _core_constants(g):
    cs = CHUNKSETS[g]
    c = {}
    own_pos = np.concatenate([np.arange(cc * CHUNK, (cc + 1) * CHUNK) for cc in cs])
    c["cosq"], c["sinq"] = _rope_tables(own_pos)
    c["cosk"], c["sink"] = _rope_tables(np.arange(S))
    ct = np.zeros((NCT, 2, 128, 256), np.float32)
    idx = 0
    for s, cc in enumerate(cs):
        for kc in range(PADC[s]):
            for sub in range(2):
                if kc == cc:
                    key_in = sub * 128 + np.arange(128)
                    ct[idx + kc, sub] = np.where(
                        np.arange(256)[None, :] >= key_in[:, None], 0.0, NEG)
                elif kc > cc:
                    ct[idx + kc, sub] = NEG
        idx += PADC[s]
    c["ctile"] = ct.astype(BF)
    pm = np.zeros((8, NCH), np.float32)
    dg = np.zeros((8, NCH), np.float32)
    for st in range(8):
        qc = cs[st // 2]
        pm[st, qc:] = NEG
        dg[st, qc] = 1.0
    c["pm"] = np.broadcast_to(pm[None], (128, 8, NCH)).copy()
    c["diag"] = np.broadcast_to(dg[None], (128, 8, NCH)).copy()
    return c



def build_program(dbg=()):
    dbg = set(dbg)
    nc = bass.Bass("TRN2", target_bir_lowering=False)

    def din(name, shape, dtype):
        return nc.dram_tensor(name, list(shape), dtype, kind="ExternalInput").ap()

    x_own = din("x_own", (TCORE, H), F32)
    x_ctx = din("x_ctx", (S, H), F32)
    wq_in = din("wq", (H, NH * D), F32)
    wk_in = din("wk", (H, NKV * D), F32)
    wv_in = din("wv", (H, NKV * D), F32)
    wo_in = din("wo", (NH * D, H), F32)
    nw_in = din("norm_w", (H,), F32)
    gwT_in = din("gwT", (H, E), F32)
    w13_in = din("w13", (E, H, 2 * II), F32)
    w2_in = din("w2", (E, II, H), F32)
    sg_in = din("sgw", (H, II), F32)
    su_in = din("suw", (H, II), F32)
    sd_in = din("sdw", (II, H), F32)
    ident_in = din("ident_bf", (128, 128), BF16)
    onesrow_in = din("ones_row", (1, 128), BF16)
    onescol_in = din("ones_col", (128, 1), BF16)
    ltones_in = din("lt_ones", (128, 128), BF16)
    ltstrict_in = din("lt_strict", (128, 128), BF16)
    eiota_in = din("e_iota", (128, E), F32)
    payload_in = din("payload", (128, NTT, TOPK, 2), I32)
    tprefill_in = din("t_prefill", (NTROW, 2), I32)
    cosq_in = din("cosq", (128, TCORE), BF16)
    sinq_in = din("sinq", (128, TCORE), BF16)
    cosk_in = din("cosk", (128, S), BF16)
    sink_in = din("sink", (128, S), BF16)
    ctile_in = din("ctile", (NCT, 2, 128, 256), BF16)
    pm_in = din("pm", (128, 8, NCH), F32)
    cvec_in = din("cvec", (128, 8), F32)
    diag_in = din("diag", (128, 8, NCH), F32)

    y_out = nc.dram_tensor("y", [TCORE, H], F32, kind="ExternalOutput").ap()
    dbg_outs = {}
    for name, shape, dt in [
        ("x1", (TCORE, H), F32), ("kt", (128, NKV, S), F32),
        ("qt", (128, 16, TCORE), F32), ("v", (128, 16, NKV * D), F32),
        ("ot", (128, 16, TCORE), F32), ("bt", (128, TCORE), F32),
        ("h", (TCORE, H), F32), ("w", (128, NTT, E), F32),
        ("islot", (128, NTT, E), F32), ("routed", (TCORE, H), F32),
    ]:
        if name in dbg:
            dbg_outs[name] = nc.dram_tensor(
                "dbg_" + name, list(shape), dt, kind="ExternalOutput").ap()

    if "lnx0" in dbg:
        dbg_outs["lnx0"] = nc.dram_tensor(
            "dbg_lnx0", [128, H + 16], F32, kind="ExternalOutput").ap()
    if "ge0" in dbg:
        dbg_outs["ge0"] = nc.dram_tensor(
            "dbg_ge0", [128, 22, 1024], F32, kind="ExternalOutput").ap()
    if "p00" in dbg:
        dbg_outs["p00"] = nc.dram_tensor(
            "dbg_p00", [128, 16, 256], F32, kind="ExternalOutput").ap()

    h_dram = nc.dram_tensor("h_dram", [TCORE, H], BF16, kind="Internal").ap()
    t_dram = nc.dram_tensor("t_dram", [NTROW, 2], I32, kind="Internal").ap()
    s_dram = nc.dram_tensor("s_dram", [SROWS, H], BF16, kind="Internal").ap()

    Q = {k: [] for k in ("sync", "tensor", "vector", "scalar", "gpsimd")}

    step_ctx = {}

    def step(eng, waits, emit, incs=()):
        incs = list(incs)
        waits = list(waits)
        own = step_ctx.get(eng)
        if own is not None:
            waits.append((own, own.val))
            if not any(sc is own for sc, _ in incs):
                incs.append((own, 1))
        wl = [(s.h, (s.val if v is None else v)) for (s, v) in waits]
        pl = []
        for (s, n) in incs:
            s.val += n
            pl.append((s.h, n))

        def f(e, wl=wl, pl=pl, emit=emit):
            for hh, v in wl:
                e.wait_ge(hh, v)
            inst = emit(e)
            for hh, n in pl:
                inst.then_inc(hh, n)
        Q[eng].append(f)

    ARENA_W = int(os.environ.get('KERNEL_ARENA_W', 103 * 1024))

    with contextlib.ExitStack() as _stk:
        arena_t = _stk.enter_context(nc.sbuf_tensor("arena", [128, ARENA_W], BF16))
        s_LD, s_GP, s_PE, s_DV, s_AC, s_ST, s_SA, s_SB, s_SC, s_SD = (
            _stk.enter_context(nc.semaphore(nm))
            for nm in ("LD", "GP", "PE", "DV", "AC", "ST", "SA", "SB", "SC", "SD"))
        pb = [_stk.enter_context(nc.psum_tensor(f"pb{i}", [128, 512], F32))
              for i in range(8)]
        block = _stk.enter_context(nc.Block())
        A = Arena(arena_t, ARENA_W)
        LD, GP = SemCtr(s_LD), SemCtr(s_GP)
        PE, DV, AC, STo = SemCtr(s_PE), SemCtr(s_DV), SemCtr(s_AC), SemCtr(s_ST)
        SL = [SemCtr(s_SA), SemCtr(s_SB), SemCtr(s_SC), SemCtr(s_SD)]
        step_ctx["vector"] = DV
        step_ctx["scalar"] = AC
        bank_fill = [0] * 8
        bank_evac = [(DV, 0)] * 8

        def mm(bank_ap, lhsT, rhs, start, stop, waits=(), inc=False):
            step("tensor", list(waits),
                 lambda t, o=bank_ap, l=lhsT, r=rhs, st=start, sp=stop:
                 t.matmul(o, l, r, start=st, stop=sp),
                 incs=([(PE, 1)] if inc else []))

        def phase_barrier():
            sems = [(LD, None), (GP, None), (PE, None), (DV, None), (AC, None),
                    (STo, None)] + [(sl_, None) for sl_ in SL]
            for eng in ("sync", "tensor", "vector", "scalar", "gpsimd"):
                step(eng, list(sems), lambda e: e.nop())

        c_ident = A.alloc_r("ident", (128, 128), BF16)
        c_onesrow = A.alloc_r("onesrow", (1, 128), BF16)
        c_onescol = A.alloc_r("onescol", (128, 1), BF16)
        c_ltones = A.alloc_r("ltones", (128, 128), BF16)
        c_ltstrict = A.alloc_r("ltstrict", (128, 128), BF16)
        c_eiota = A.alloc_r("eiota", (128, E), F32)
        c_payload = A.alloc_r("payload", (128, NTT, TOPK, 2), I32)
        c_cosq = A.alloc("cosq", (128, TCORE), BF16)
        c_sinq = A.alloc("sinq", (128, TCORE), BF16)
        c_cosk = A.alloc("cosk", (128, S), BF16)
        c_sink = A.alloc("sink", (128, S), BF16)
        c_pm = A.alloc_r("pm", (128, 8, NCH), F32)
        c_diag = A.alloc_r("diag", (128, 8, NCH), F32)
        c_nw = A.alloc_r("nw", (128, H), F32)

        for dst, src in [
            (c_ident, ident_in), (c_onesrow, onesrow_in), (c_onescol, onescol_in),
            (c_ltones, ltones_in), (c_ltstrict, ltstrict_in), (c_eiota, eiota_in),
            (c_payload, payload_in), (c_cosq, cosq_in), (c_sinq, sinq_in),
            (c_cosk, cosk_in), (c_sink, sink_in), (c_pm, pm_in), (c_diag, diag_in),
        ]:
            step("sync", [], lambda e, d=dst, s=src: e.dma_start(out=d, in_=s),
                 [(LD, 16)])
        step("sync", [],
             lambda e: e.dma_start(out=c_nw,
                                   in_=nw_in.unsqueeze(0).broadcast_to([128, H])),
             [(LD, 16)])
        consts_done = LD.val

        step("gpsimd", [],
             lambda e: e.dma_start(out=t_dram, in_=tprefill_in), [(GP, 16)])
        zero_w = A.alloc("zrow", (128, 2048), BF16)
        step("vector", [], lambda e: e.memset(zero_w, 0.0), [(DV, 1)])
        zrow_done = DV.val
        NZTILE = (1 + TCORE * TOPK + 127) // 128
        for zi in range(NZTILE):
            r0 = zi * 128
            rows = min(128, 1 + TCORE * TOPK - r0)
            step("gpsimd", [(DV, zrow_done)],
                 lambda e, r0=r0, rows=rows, zv=zero_w: e.dma_start(
                     out=s_dram[r0:r0 + rows, :], in_=zv[0:rows, :]),
                 [(GP, 16)])
        prep_gp = GP.val

        sb_wk = A.alloc("wk", (128, 16, NKV * D), BF16)
        sb_wv = A.alloc("wv", (128, 16, NKV * D), BF16)
        for hb in range(16):
            step("gpsimd", [],
                 lambda e, hb=hb: e.dma_start(
                     out=sb_wk[:, hb, :], in_=wk_in[hb * 128:(hb + 1) * 128, :]),
                 [(GP, 16)])
            step("gpsimd", [],
                 lambda e, hb=hb: e.dma_start(
                     out=sb_wv[:, hb, :], in_=wv_in[hb * 128:(hb + 1) * 128, :]),
                 [(GP, 16)])
        kvw_done = GP.val

        sb_stats = A.alloc_r("stats", (128, 16), F32)

        def emit_ln1(x_ap, lnx_ap, scol):
            m = sb_stats[:, scol:scol + 1]
            v = sb_stats[:, scol + 1:scol + 2]
            t = sb_stats[:, scol + 2:scol + 3]
            step("vector", [], lambda e: e.reduce_sum(m, x_ap, axis=AX), [(DV, 1)])
            step("vector", [(DV, None)],
                 lambda e: e.tensor_scalar(m, m, 1.0 / H, None, OP.mult),
                 [(DV, 1)])
            dv_m = DV.val
            step("scalar", [],
                 lambda e: e.activation(out=lnx_ap, in_=x_ap, func=AF.Square,
                                        accum_out=v), [(AC, 1)])
            ac_sq = AC.val
            step("vector", [(AC, ac_sq), (DV, dv_m)],
                 lambda e: e.tensor_scalar(v, v, 1.0 / H, None, OP.mult), [(DV, 1)])
            step("vector", [], lambda e: e.tensor_mul(t, m, m), [(DV, 1)])
            step("vector", [], lambda e: e.tensor_sub(v, v, t), [(DV, 1)])
            step("vector", [], lambda e: e.tensor_scalar(v, v, EPS1, None, OP.add),
                 [(DV, 1)])
            dv_v = DV.val
            step("scalar", [(DV, dv_v)],
                 lambda e: e.activation(out=v, in_=v, func=AF.Sqrt), [(AC, 1)])
            ac_r = AC.val
            step("vector", [(AC, ac_r)],
                 lambda e: e.reciprocal(v, v), [(DV, 1)])
            step("vector", [],
                 lambda e: e.tensor_scalar(lnx_ap, x_ap, m, v, OP.subtract, OP.mult),
                 [(DV, 1)])
            return DV.val

        def transpose_tile(src_ap, dst_ap, bank, boff):
            pv = pb[bank].bitcast(BF16)[:, boff * 128:(boff + 1) * 128]
            ev_s, ev_v = bank_evac[bank]
            step("tensor", [(ev_s, ev_v)],
                 lambda t, s=src_ap, p=pv: t.transpose(p, s, c_ident), [(PE, 1)])
            bank_fill[bank] = PE.val
            step("vector", [(PE, bank_fill[bank])],
                 lambda e, d=dst_ap, p=pv: e.tensor_copy(d, p), [(DV, 1)])
            bank_evac[bank] = (DV, DV.val)

        def rope_from_psum(psum_ap, cos_ap, sin_ap, out_ap, w):
            t1 = A.view("ropetmp")[:, 0:w]
            t2 = A.view("ropetmp")[:, 512:512 + w]
            step("vector", [],
                 lambda e: e.tensor_mul(t1[0:64], psum_ap[0:64], cos_ap[0:64]),
                 [(DV, 1)])
            step("vector", [],
                 lambda e: e.tensor_mul(t2[0:64], psum_ap[64:128], sin_ap[0:64]),
                 [(DV, 1)])
            step("vector", [],
                 lambda e: e.tensor_sub(out_ap[0:64], t1[0:64], t2[0:64]), [(DV, 1)])
            step("vector", [],
                 lambda e: e.tensor_mul(t1[0:64], psum_ap[64:128], cos_ap[64:128]),
                 [(DV, 1)])
            step("vector", [],
                 lambda e: e.tensor_mul(t2[0:64], psum_ap[0:64], sin_ap[64:128]),
                 [(DV, 1)])
            step("vector", [],
                 lambda e: e.tensor_add(out_ap[64:128], t1[0:64], t2[0:64]),
                 [(DV, 1)])
            return DV.val

        A.alloc("ropetmp", (64, 1024), F32)
        if "lnx0" in dbg:
            A.alloc("dbglnx", (128, H + 16), F32)

        sb_x = A.alloc("xs", (128, 2, H), F32)
        sb_lnx = A.alloc("lnxs", (128, 2, H), BF16)
        sb_lnxTc = A.alloc("lnxTc", (128, 1, 16, 512), BF16)
        sb_KT = A.alloc_r("KT", (128, NKV, S), BF16)
        sb_V = A.alloc_r("V", (128, 16, NKV * D), BF16)
        x_free = [(DV, 0), (DV, 0)]
        lnx_free = [(PE, 0), (PE, 0)]
        lnxTc_free = [(PE, 0)]

        for cchunk in range(4):
            cslot = 0
            for tt4 in range(4):
                tt = cchunk * 4 + tt4
                slot = tt % 2
                step("sync", [x_free[slot], (LD, None)],
                     lambda e, tt=tt, slot=slot: e.dma_start(
                         out=sb_x[:, slot, :],
                         in_=x_ctx[tt * 128:(tt + 1) * 128, :]),
                     [(SL[slot], 16)])
                ld_x = SL[slot].val
                step("vector", [(SL[slot], ld_x)], lambda e: e.nop())
                step("scalar", [(SL[slot], ld_x), lnx_free[slot]],
                     lambda e: e.nop())
                dv_ln = emit_ln1(sb_x[:, slot, :], sb_lnx[:, slot, :], 4 * slot)
                x_free[slot] = (DV, dv_ln)
                if tt == 0 and "lnx0" in dbg_outs:
                    step("vector", [(DV, dv_ln)],
                         lambda e: e.tensor_copy(A.view("dbglnx")[:, 0:H],
                                                 sb_lnx[:, 0, :]), [(DV, 1)])
                    step("vector", [],
                         lambda e: e.tensor_copy(A.view("dbglnx")[:, H:H + 16],
                                                 sb_stats), [(DV, 1)])
                step("tensor", [(DV, dv_ln), lnxTc_free[cslot]], lambda e: e.nop())
                for hb in range(16):
                    transpose_tile(
                        sb_lnx[:, slot, hb * 128:(hb + 1) * 128],
                        sb_lnxTc[:, cslot, hb, tt4 * 128:(tt4 + 1) * 128],
                        6 + (hb // 8), hb % 8)
                lnx_free[slot] = (PE, PE.val)
            chunk_ready = DV.val
            for dkv in range(NKV):
                bank = dkv % 2
                ev_s, ev_v = bank_evac[bank]
                step("tensor", [(ev_s, ev_v), (DV, chunk_ready),
                                (GP, kvw_done)], lambda e: e.nop())
                for hb in range(16):
                    mm(pb[bank][:, :], sb_wk[:, hb, dkv * 128:(dkv + 1) * 128],
                       sb_lnxTc[:, cslot, hb, :], hb == 0, hb == 15,
                       inc=(hb == 15))
                bank_fill[bank] = PE.val
                cs0 = cchunk * 512
                step("vector", [(PE, bank_fill[bank])], lambda e: e.nop())
                dvv = rope_from_psum(pb[bank][:, :], c_cosk[:, cs0:cs0 + 512],
                                     c_sink[:, cs0:cs0 + 512],
                                     sb_KT[:, dkv, cs0:cs0 + 512], 512)
                bank_evac[bank] = (DV, dvv)
            for vt in range(4):
                bank = 2 + vt % 2
                ev_s, ev_v = bank_evac[bank]
                step("tensor", [(ev_s, ev_v)], lambda e: e.nop())
                for hb in range(16):
                    mm(pb[bank][:, :], sb_lnxTc[:, cslot, hb, vt * 128:(vt + 1) * 128],
                       sb_wv[:, hb, :], hb == 0, hb == 15, inc=(hb == 15))
                bank_fill[bank] = PE.val
                vidx = cchunk * 4 + vt
                step("vector", [(PE, bank_fill[bank])],
                     lambda e, vidx=vidx, bank=bank: e.tensor_copy(
                         sb_V[:, vidx, :], pb[bank][:, :]), [(DV, 1)])
                bank_evac[bank] = (DV, DV.val)
            lnxTc_free[cslot] = (PE, PE.val)

        sb_lnxT_own = A.alloc_r("lnxTown", (128, 16, TCORE), BF16)
        for tt in range(NTT):
            slot = tt % 2
            step("sync", [x_free[slot]],
                 lambda e, tt=tt, slot=slot: e.dma_start(
                     out=sb_x[:, slot, :], in_=x_own[tt * 128:(tt + 1) * 128, :]),
                 [(SL[slot], 16)])
            ld_x = SL[slot].val
            step("vector", [(SL[slot], ld_x)], lambda e: e.nop())
            step("scalar", [(SL[slot], ld_x), lnx_free[slot]], lambda e: e.nop())
            dv_ln = emit_ln1(sb_x[:, slot, :], sb_lnx[:, slot, :], 4 * slot)
            x_free[slot] = (DV, dv_ln)
            step("tensor", [(DV, dv_ln)], lambda e: e.nop())
            for hb in range(16):
                transpose_tile(sb_lnx[:, slot, hb * 128:(hb + 1) * 128],
                               sb_lnxT_own[:, hb, tt * 128:(tt + 1) * 128],
                               6 + (hb // 8), hb % 8)
            lnx_free[slot] = (PE, PE.val)
        A.release("xs", "lnxs", "lnxTc")
        phase_barrier()

        sb_QT = A.alloc_r("QT", (128, 16, TCORE), BF16)
        sb_wq_s = [A.alloc("wqA", (128, 16, 512), BF16),
                   A.alloc("wqB", (128, 16, 512), BF16)]
        wq_slot_free = [(PE, 0), (PE, 0)]
        wq_ld = [0, 0]
        for quarter in range(4):
            ws = quarter % 2
            step("gpsimd", [wq_slot_free[ws]], lambda e: e.nop())
            for hb in range(16):
                step("gpsimd", [],
                     lambda e, hb=hb, quarter=quarter, ws=ws: e.dma_start(
                         out=sb_wq_s[ws][:, hb, :],
                         in_=wq_in[hb * 128:(hb + 1) * 128,
                                   quarter * 512:(quarter + 1) * 512]),
                     [(GP, 16)])
            wq_ld[ws] = GP.val
            for dq4 in range(4):
                dq = quarter * 4 + dq4
                for tc in range(4):
                    bank = dq4
                    half = tc % 2
                    bank = (dq4 % 4)
                    pv = pb[bank][:, half * 256:(half + 1) * 256]
                    ev_s, ev_v = bank_evac[bank]
                    step("tensor", [(ev_s, ev_v), (GP, wq_ld[ws])],
                         lambda e: e.nop())
                    for hb in range(16):
                        mm(pv, sb_wq_s[ws][:, hb, dq4 * 128:(dq4 + 1) * 128],
                           sb_lnxT_own[:, hb, tc * 256:(tc + 1) * 256],
                           hb == 0, hb == 15, inc=(hb == 15))
                    bank_fill[bank] = PE.val
                    step("vector", [(PE, bank_fill[bank])], lambda e: e.nop())
                    dvv = rope_from_psum(
                        pv, c_cosq[:, tc * 256:(tc + 1) * 256],
                        c_sinq[:, tc * 256:(tc + 1) * 256],
                        sb_QT[:, dq, tc * 256:(tc + 1) * 256], 256)
                    bank_evac[bank] = (DV, dvv)
            wq_slot_free[ws] = (PE, PE.val)
        A.release("wqA", "wqB", "lnxTown", "wk", "wv")
        phase_barrier()

        c_ctile = A.alloc("ctile", (128, NCT * 2, 256), BF16)
        c_cvec = A.alloc_r("cvec", (128, 8), F32)
        for i in range(NCT * 2):
            step("sync", [], lambda e, i=i: e.dma_start(
                out=c_ctile[:, i, :],
                in_=ctile_in.rearrange("a b p q -> (a b) p q")[i]), [(LD, 16)])
        step("sync", [], lambda e: e.dma_start(out=c_cvec, in_=cvec_in), [(LD, 16)])
        ld_ctile = LD.val

        sb_km = A.alloc("kmean", (128, NKV, NCH), F32)
        sb_kmb = A.alloc("kmeanb", (128, NKV, NCH), BF16)
        kt_done = DV.val
        for kv in range(NKV):
            for n in range(NCH):
                step("vector", [], lambda e, kv=kv, n=n: e.reduce_sum(
                    sb_km[:, kv, n:n + 1], sb_KT[:, kv, n * 256:(n + 1) * 256],
                    axis=AX), [(DV, 1)])
        step("vector", [(DV, None)],
             lambda e: e.tensor_copy(sb_kmb, sb_km), [(DV, 1)])
        km_done = DV.val

        sb_BT = A.alloc_r("BT", (128, TCORE), BF16)
        sb_gsc = A.alloc("gsc", (128, 4, NCH), F32)
        sb_Ball = A.alloc("Ball", (128, 128), BF16)
        ball_pe = 0
        for st4 in range(8):
            gbank = 4
            ev_s, ev_v = bank_evac[gbank]
            step("tensor", [(ev_s, ev_v), (DV, km_done)], lambda e: e.nop())
            for hh in range(16):
                mm(pb[gbank][:, hh * 8:hh * 8 + 8],
                   sb_QT[:, hh, st4 * 128:(st4 + 1) * 128],
                   sb_kmb[:, hh // 4, :], True, True, inc=(hh == 15))
            bank_fill[gbank] = PE.val
            bdv0 = DV.val
            for hh in range(16):
                gp = sb_gsc[:, 0, :]
                t8 = sb_gsc[:, 1, :]
                sel = sb_gsc[:, 2, :]
                step("vector",
                     [(PE, bank_fill[gbank]), (PE, ball_pe)] if hh == 0 else [],
                     lambda e, hh=hh, gp=gp, st4=st4: e.scalar_tensor_tensor(
                         out=gp, in0=pb[4][:, hh * 8:hh * 8 + 8], scalar=1.0,
                         in1=c_pm[:, st4, :], op0=OP.mult, op1=OP.add), [(DV, 1)])
                step("vector", [], lambda e, gp=gp, t8=t8: e.max(t8, gp), [(DV, 1)])
                step("vector", [(DV, None)],
                     lambda e, gp=gp, t8=t8, sel=sel: e.tensor_scalar(
                         sel, gp, t8[:, 2:3], None, OP.is_ge), [(DV, 1)])
                step("vector", [], lambda e, sel=sel, st4=st4: e.tensor_max(
                    sel, sel, c_diag[:, st4, :]), [(DV, 1)])
                step("vector", [], lambda e, sel=sel, hh=hh: e.tensor_scalar(
                    sb_Ball[:, hh * 8:hh * 8 + 8], sel, 1e9, -1e9,
                    OP.mult, OP.add), [(DV, 1)])
            ball_done = DV.val
            bank_evac[gbank] = (DV, ball_done)
            tb = 5
            ev_s, ev_v = bank_evac[tb]
            step("tensor", [(ev_s, ev_v), (DV, ball_done)],
                 lambda t: t.transpose(pb[tb].bitcast(BF16)[:, 0:128], sb_Ball,
                                       c_ident), [(PE, 1)])
            bank_fill[tb] = PE.val
            ball_pe = PE.val
            step("vector", [(PE, bank_fill[tb])],
                 lambda e, st4=st4: e.tensor_copy(
                     sb_BT[:, st4 * 128:(st4 + 1) * 128],
                     pb[tb].bitcast(BF16)[:, 0:128]), [(DV, 1)])
            bank_evac[tb] = (DV, DV.val)
        bt_done = DV.val

        sb_OT = A.alloc_r("OT", (128, 16, TCORE), BF16)
        if "p00" in dbg_outs:
            A.alloc_r("dbgp", (128, 16, 256), F32)
        sb_B0 = A.alloc_r("B0", (1, 2, NCH, 256), BF16)
        sb_probs = A.alloc("probs", (128, 4, 256), BF16)
        sb_dn = A.alloc("dn", (1, 2, 256), BF16)
        sb_dnf = A.alloc("dnf", (1, 2, 256), F32)
        sb_otr = A.alloc("otr", (128, 2, 256), BF16)
        probs_free = [(PE, 0)] * 4
        probs_ready = [0] * 4
        SCOB = [0, 1, 6]
        sco_free = [bank_evac[b] for b in SCOB]
        sidx = 0
        b0_free = [(PE, 0), (PE, 0)]
        b0_ld = [0, 0]
        step("tensor", [(LD, ld_ctile), (DV, bt_done)], lambda e: e.nop())
        for hh in range(16):
            kvh = hh // 4
            for st4 in range(4):
                qs = st4 * 256
                par = (hh * 4 + st4) % 2
                bsl = par
                step("sync", [b0_free[bsl], (DV, bt_done)],
                     lambda e, hh=hh, bsl=bsl, qs=qs: e.dma_start(
                         out=sb_B0[:, bsl, :, :],
                         in_=sb_BT[hh * 8:(hh + 1) * 8, qs:qs + 256]),
                     [(SL[bsl], 16)])
                b0_ld[bsl] = SL[bsl].val
                step("tensor", [(SL[bsl], b0_ld[bsl])], lambda e: e.nop())
                ob = 2 + par
                db = 4 + par
                opsum = pb[ob][:, 0:256]
                dpsum = pb[db][0:1, 0:256]
                rep = pb[db][:, 256:512]
                ctoff = sum(PADC[:st4])
                nsub = PADC[st4] * 2
                step("tensor", [bank_evac[ob], bank_evac[db]], lambda e: e.nop())
                for si in range(nsub):
                    kc, sub = si // 2, si % 2
                    sl = sidx % 3
                    sidx += 1
                    spsum = pb[SCOB[sl]][:, 0:256]
                    step("tensor", [sco_free[sl]], lambda e: e.nop())
                    mm(spsum, c_ident, c_ctile[:, (ctoff + kc) * 2 + sub, :],
                       True, False)
                    mm(spsum, c_onesrow, sb_B0[:, bsl, kc, :],
                       False, False)
                    mm(spsum, sb_KT[:, kvh, (kc * 2 + sub) * 128:(kc * 2 + sub + 1) * 128],
                       sb_QT[:, hh, qs:qs + 256], False, True, inc=True)
                    s_fill = PE.val
                    step("scalar", [(PE, s_fill), probs_free[sl]],
                         lambda e, sl=sl, spsum=spsum: e.activation(
                             out=sb_probs[:, sl, :], in_=spsum, func=AF.Exp,
                             scale=SCALE), [(AC, 1)])
                    probs_ready[sl] = AC.val
                    sco_free[sl] = (AC, AC.val)
                    if "p00" in dbg_outs and hh == 0 and st4 == 1:
                        step("vector", [(AC, AC.val)],
                             lambda e, si=si, sl=sl: e.tensor_copy(
                                 A.view("dbgp")[:, si, :], sb_probs[:, sl, :]),
                             [(DV, 1)])
                        step("vector", [(AC, AC.val)],
                             lambda e, si=si, spsum=spsum: e.tensor_copy(
                                 A.view("dbgp")[:, 8 + si, :], spsum), [(DV, 1)])
                    step("tensor", [(AC, probs_ready[sl])], lambda e: e.nop())
                    mm(opsum, sb_V[:, kc * 2 + sub, kvh * 128:(kvh + 1) * 128],
                       sb_probs[:, sl, :], si == 0, si == nsub - 1)
                    mm(dpsum, c_onescol, sb_probs[:, sl, :], si == 0,
                       si == nsub - 1, inc=(si == nsub - 1))
                    probs_free[sl] = (PE, PE.val)
                d_fill = PE.val
                step("vector", [(PE, d_fill)],
                     lambda e, par=par, dpsum=dpsum: e.reciprocal(
                         sb_dnf[:, par, :], dpsum), [(DV, 1)])
                step("vector", [(DV, None)],
                     lambda e, par=par: e.tensor_copy(
                         sb_dn[:, par, :], sb_dnf[:, par, :]), [(DV, 1)])
                if "p00" in dbg_outs and hh == 0 and st4 in (1, 2):
                    step("vector", [(DV, None)],
                         lambda e, st4=st4, par=par: e.tensor_copy(
                             A.view("dbgp")[0:1, 13 + st4, :],
                             sb_dnf[:, par, :]), [(DV, 1)])
                rc_done = DV.val
                step("tensor", [(DV, rc_done)], lambda e: e.nop())
                mm(rep, c_onesrow, sb_dn[:, par, :], True, True, inc=True)
                rep_fill = PE.val
                step("vector", [(PE, rep_fill)],
                     lambda e, par=par, opsum=opsum: e.tensor_copy(
                         sb_otr[:, par, :], opsum), [(DV, 1)])
                bank_evac[ob] = (DV, DV.val)
                step("vector", [(DV, None)],
                     lambda e, hh=hh, qs=qs, par=par, rep=rep: e.tensor_mul(
                         sb_OT[:, hh, qs:qs + 256], sb_otr[:, par, :], rep),
                     [(DV, 1)])
                bank_evac[db] = (DV, DV.val)
                b0_free[bsl] = (PE, PE.val)
        ot_done = DV.val
        for _i, _b in enumerate(SCOB):
            bank_evac[_b] = sco_free[_i]
        _dbg_keep = {"qt": "QT", "kt": "KT", "v": "V", "bt": "BT"}
        _rel = ["probs", "dn", "dnf", "otr", "gsc", "Ball", "ctile", "kmean",
                "kmeanb", "cosq", "sinq", "cosk", "sink"]
        for _k, _nm in _dbg_keep.items():
            if _k not in dbg_outs:
                _rel.append(_nm)
        A.release(*_rel)
        phase_barrier()

        sb_wo_h = [A.alloc("wo_a", (128, 8, H), BF16),
                   A.alloc("wo_b", (128, 8, H), BF16)]

        def wo_slice(hh, ho):
            return sb_wo_h[hh // 8][:, hh % 8, ho * 512:(ho + 1) * 512]
        for hb in range(16):
            step("gpsimd", [], lambda e, hb=hb: e.dma_start(
                out=sb_wo_h[hb // 8][:, hb % 8, :],
                in_=wo_in[hb * 128:(hb + 1) * 128, :]), [(GP, 16)])
        wo_ld = GP.val
        sb_xe = A.alloc("xe", (128, 2, H), F32)
        sb_x1 = A.alloc_r("x1", (128, NTT, H), BF16)
        sb_junk = A.alloc("junk", (128, H), BF16)
        sb_s2 = A.alloc_r("s2", (128, NTT, 4), F32)
        xe_free = [(DV, 0), (DV, 0)]
        step("tensor", [(GP, wo_ld), (DV, ot_done)], lambda e: e.nop())
        for tt in range(NTT):
            slot = tt % 2
            step("sync", [xe_free[slot]],
                 lambda e, tt=tt, slot=slot: e.dma_start(
                     out=sb_xe[:, slot, :], in_=x_own[tt * 128:(tt + 1) * 128, :]),
                 [(SL[2 + slot], 16)])
            ld_xe = SL[2 + slot].val
            for ho in range(4):
                bank = ho % 2
                ev_s, ev_v = bank_evac[bank]
                step("tensor", [(ev_s, ev_v)], lambda e: e.nop())
                for hh in range(16):
                    mm(pb[bank][:, :], sb_OT[:, hh, tt * 128:(tt + 1) * 128],
                       wo_slice(hh, ho), hh == 0, hh == 15, inc=(hh == 15))
                bank_fill[bank] = PE.val
                step("vector", [(PE, bank_fill[bank]), (SL[2 + slot], ld_xe)],
                     lambda e, tt=tt, ho=ho, slot=slot, bank=bank:
                     e.scalar_tensor_tensor(
                         out=sb_x1[:, tt, ho * 512:(ho + 1) * 512],
                         in0=pb[bank][:, :], scalar=1.0,
                         in1=sb_xe[:, slot, ho * 512:(ho + 1) * 512],
                         op0=OP.mult, op1=OP.add), [(DV, 1)])
                bank_evac[bank] = (DV, DV.val)
            xe_free[slot] = (DV, DV.val)
            x1row_done = DV.val
            m2 = sb_s2[:, tt, 0:1]
            r1 = sb_s2[:, tt, 1:2]
            rho = sb_s2[:, tt, 2:3]
            tmp = sb_s2[:, tt, 3:4]
            step("vector", [], lambda e, tt=tt, m2=m2: e.reduce_sum(
                m2, sb_x1[:, tt, :], axis=AX), [(DV, 1)])
            step("vector", [(DV, None)],
                 lambda e, m2=m2: e.tensor_scalar(m2, m2, 1.0 / H, None, OP.mult),
                 [(DV, 1)])
            dv_m2 = DV.val
            step("scalar", [(DV, x1row_done)],
                 lambda e, tt=tt, tmp=tmp: e.activation(
                     out=sb_junk, in_=sb_x1[:, tt, :], func=AF.Square,
                     accum_out=tmp), [(AC, 1)])
            ac_ss = AC.val
            step("vector", [(AC, ac_ss), (DV, dv_m2)],
                 lambda e, tmp=tmp: e.tensor_scalar(tmp, tmp, 1.0 / H, None,
                                                    OP.mult), [(DV, 1)])
            step("vector", [], lambda e, m2=m2, r1=r1: e.tensor_mul(r1, m2, m2),
                 [(DV, 1)])
            step("vector", [], lambda e, tmp=tmp, r1=r1: e.tensor_sub(tmp, tmp, r1),
                 [(DV, 1)])
            step("vector", [], lambda e, tmp=tmp: e.tensor_scalar(
                tmp, tmp, EPS1, None, OP.add), [(DV, 1)])
            dv_var = DV.val
            step("scalar", [(DV, dv_var)],
                 lambda e, tmp=tmp, r1=r1: e.activation(out=r1, in_=tmp,
                                                        func=AF.Sqrt), [(AC, 1)])
            step("vector", [(AC, None)],
                 lambda e, r1=r1: e.reciprocal(r1, r1), [(DV, 1)])
            step("vector", [(DV, None)],
                 lambda e, tmp=tmp, rho=rho: e.reciprocal(rho, tmp), [(DV, 1)])
            step("vector", [(DV, None)],
                 lambda e, rho=rho: e.tensor_scalar(
                     rho, rho, -EPS1, 1.0 + EPS2, OP.mult, OP.add), [(DV, 1)])
            dv_w = DV.val
            step("scalar", [(DV, dv_w)],
                 lambda e, rho=rho: e.activation(out=rho, in_=rho, func=AF.Sqrt),
                 [(AC, 1)])
            step("vector", [(AC, None)],
                 lambda e, rho=rho: e.reciprocal(rho, rho), [(DV, 1)])
            step("vector", [(DV, None)],
                 lambda e, rho=rho, r1=r1: e.tensor_mul(rho, rho, r1), [(DV, 1)])
        x1_done = DV.val
        A.release("wo_a", "wo_b", "xe", "junk")
        phase_barrier()

        sb_h = A.alloc("hbuf", (128, 2, H), BF16)
        sb_htmp = A.alloc("htmp", (128, H), F32)
        sb_hT = A.alloc_r("hT", (128, 16, TCORE), BF16)
        sb_gw = A.alloc("gw", (128, 16, E), BF16)
        sb_mself = A.alloc_r("mself", (128, NTT, E), F32)
        sb_mselb = A.alloc_r("mselb", (128, NTT, E), BF16)
        sb_j = A.alloc_r("jranks", (128, NTT, E), F32)
        sb_w = A.alloc_r("wgt", (128, NTT, E), F32)
        sb_wt = A.alloc_r("wtk", (128, NTT, TOPK), F32)
        sb_gidx = A.alloc_r("gidx", (128, NTT, TOPK), I32)
        sb_r32 = A.alloc("r32", (128, 6, E), F32)
        dbgisl_v = None
        if "islot" in dbg_outs:
            dbgisl_v = A.alloc_r("dbgisl", (128, NTT, E), F32)
        sb_rz = A.alloc("rz", (128, E), F32)
        for hb in range(16):
            step("gpsimd", [], lambda e, hb=hb: e.dma_start(
                out=sb_gw[:, hb, :], in_=gwT_in[hb * 128:(hb + 1) * 128, :]),
                 [(GP, 16)])
        gw_ld = GP.val
        step("vector", [], lambda e: e.memset(sb_rz, 0.0))

        h_free = [(SL[0], 0), (SL[1], 0)]
        for tt in range(NTT):
            slot = tt % 2
            m2 = sb_s2[:, tt, 0:1]
            rho = sb_s2[:, tt, 2:3]
            step("vector", [(AC, None), h_free[slot]],
                 lambda e, tt=tt, m2=m2, rho=rho: e.tensor_scalar(
                     sb_htmp, sb_x1[:, tt, :], m2, rho, OP.subtract, OP.mult))
            step("vector", [],
                 lambda e, slot=slot: e.tensor_mul(
                     sb_h[:, slot, :], sb_htmp, c_nw))
            h_dv = DV.val
            step("sync", [(DV, h_dv)],
                 lambda e, tt=tt, slot=slot: e.dma_start(
                     out=h_dram[tt * 128:(tt + 1) * 128, :], in_=sb_h[:, slot, :]),
                 [(SL[slot], 16)])
            h_free[slot] = (SL[slot], SL[slot].val)
            if "h" in dbg_outs:
                step("gpsimd", [(DV, h_dv)],
                     lambda e, tt=tt, slot=slot: e.dma_start(
                         out=dbg_outs["h"][tt * 128:(tt + 1) * 128, :],
                         in_=sb_h[:, slot, :]), [(GP, 16)])
                h_free[slot] = (GP, GP.val)
        h_all = [SL[0].val, SL[1].val]

        for hb in range(16):
            step("sync", [(SL[0], h_all[0]), (SL[1], h_all[1])],
                 lambda e, hb=hb: e.dma_start_transpose(
                     sb_hT[:, hb, :], h_dram[:, hb * 128:(hb + 1) * 128]),
                 [(LD, 16)])
        hT_ld = LD.val

        logits = sb_r32[:, 0, :]
        t8r = sb_r32[:, 1, :]
        ntau = sb_r32[:, 5, 0:1]
        ew = sb_r32[:, 2, :]
        mk = sb_r32[:, 3, :]
        junk32 = sb_r32[:, 4, :]
        rb = 7
        for tt in range(NTT):
            ev_s, ev_v = bank_evac[rb]
            step("tensor", [(ev_s, ev_v), (LD, hT_ld), (GP, gw_ld)],
                 lambda e: e.nop())
            for hb in range(16):
                mm(pb[rb][:, 0:E], sb_hT[:, hb, tt * 128:(tt + 1) * 128],
                   sb_gw[:, hb, :], hb == 0, hb == 15, inc=(hb == 15))
            bank_fill[rb] = PE.val
            step("vector", [(PE, bank_fill[rb])],
                 lambda e: e.tensor_copy(logits, pb[rb][:, 0:E]))
            bank_evac[rb] = (DV, DV.val)
            step("vector", [], lambda e: e.max(t8r[:, 0:8], logits))
            step("vector", [], lambda e: e.tensor_scalar(
                ntau, t8r[:, 3:4], -1.0, None, OP.mult))
            step("vector", [], lambda e, tt=tt: e.tensor_scalar(
                sb_mself[:, tt, :], logits, t8r[:, 3:4], None, OP.is_ge))
            dv_msel = DV.val
            step("scalar", [(DV, dv_msel)],
                 lambda e: e.activation(out=ew, in_=logits, func=AF.Exp,
                                        bias=ntau, scale=1.0))
            ac_ew = AC.val
            step("vector", [(AC, ac_ew)],
                 lambda e, tt=tt: e.tensor_mul(ew, ew, sb_mself[:, tt, :]))
            step("vector", [], lambda e: e.reduce_sum(
                sb_r32[:, 5, 1:2], ew, axis=AX))
            step("vector", [], lambda e: e.reciprocal(
                sb_r32[:, 5, 2:3], sb_r32[:, 5, 1:2]))
            step("vector", [], lambda e, tt=tt: e.tensor_scalar(
                sb_w[:, tt, :], ew, sb_r32[:, 5, 2:3], None, OP.mult))
            step("vector", [], lambda e, tt=tt: e.tensor_tensor_scan(
                sb_j[:, tt, :], sb_mself[:, tt, :], sb_rz, 0.0, OP.add, OP.add))
            step("vector", [], lambda e, tt=tt: e.tensor_sub(
                sb_j[:, tt, :], sb_j[:, tt, :], sb_mself[:, tt, :]))
            step("vector", [], lambda e, tt=tt: e.tensor_copy(
                sb_mselb[:, tt, :], sb_mself[:, tt, :]))
        rt_done = DV.val

        sb_isl = A.alloc("isl", (128, E), F32)
        sb_kcols = A.alloc("kcols", (128, 8), F32)
        cb = 7
        tsc_gp0 = GP.val
        for tt in range(NTT):
            ev_s, ev_v = bank_evac[cb]
            step("tensor", [(ev_s, ev_v), (DV, rt_done)], lambda e: e.nop())
            for tp in range(tt + 1):
                mm(pb[cb][:, 0:E],
                   c_ltstrict if tp == tt else c_ltones,
                   sb_mselb[:, tp, :], tp == 0, tp == tt, inc=(tp == tt))
            bank_fill[cb] = PE.val
            step("vector", [(PE, bank_fill[cb])],
                 lambda e: e.tensor_copy(sb_isl, pb[cb][:, 0:E]))
            bank_evac[cb] = (DV, DV.val)
            step("vector", [], lambda e: e.tensor_scalar(
                junk32, sb_isl, float(CAP), None, OP.is_lt))
            step("vector", [], lambda e, tt=tt: e.tensor_mul(
                junk32, junk32, sb_mself[:, tt, :]))
            m2cap_dv = DV.val
            for k in range(TOPK):
                step("vector", [], lambda e, tt=tt, k=k: e.tensor_scalar(
                    mk, sb_j[:, tt, :], float(k), None, OP.is_equal))
                step("vector", [], lambda e: e.tensor_mul(mk, mk, junk32))
                sk = sb_kcols[:, 0:1]; ek = sb_kcols[:, 1:2]
                vk = sb_kcols[:, 2:3]; gf = sb_kcols[:, 3:4]
                u1 = sb_kcols[:, 4:5]
                step("vector", [], lambda e, sk=sk: e.scalar_tensor_tensor(
                    out=ew, in0=sb_isl, scalar=1.0, in1=mk,
                    op0=OP.mult, op1=OP.mult, accum_out=sk))
                step("vector", [], lambda e, ek=ek: e.scalar_tensor_tensor(
                    out=ew, in0=c_eiota, scalar=1.0, in1=mk,
                    op0=OP.mult, op1=OP.mult, accum_out=ek))
                step("vector", [], lambda e, vk=vk: e.reduce_sum(vk, mk, axis=AX))
                step("vector", [], lambda e, tt=tt, k=k: e.scalar_tensor_tensor(
                    out=ew, in0=sb_w[:, tt, :], scalar=1.0, in1=mk,
                    op0=OP.mult, op1=OP.mult, accum_out=sb_wt[:, tt, k:k + 1]))
                trash = float(E * (CAP + 1) + tt * TOPK + k)
                step("vector", [], lambda e, ek=ek, gf=gf: e.tensor_scalar(
                    gf, ek, 257.0, None, OP.mult))
                step("vector", [], lambda e, gf=gf, sk=sk: e.tensor_add(gf, gf, sk))
                step("vector", [], lambda e, gf=gf, vk=vk: e.tensor_mul(gf, gf, vk))
                step("vector", [], lambda e, vk=vk, u1=u1, trash=trash:
                     e.tensor_scalar(u1, vk, -trash, trash, OP.mult, OP.add))
                step("vector", [], lambda e, gf=gf, u1=u1: e.tensor_add(gf, gf, u1))
                step("vector", [], lambda e, tt=tt, k=k, gf=gf: e.tensor_copy(
                    sb_gidx[:, tt, k:k + 1], gf))
            if "islot" in dbg_outs:
                step("vector", [], lambda e, tt=tt, dv_=dbgisl_v: e.tensor_copy(
                    dv_[:, tt, :], sb_isl))
        slots_dv = DV.val
        for tt in range(NTT):
            for k in range(TOPK):
                step("gpsimd", [(DV, slots_dv), (GP, tsc_gp0)],
                     lambda e, tt=tt, k=k: e.indirect_dma_start(
                         t_dram,
                         IndirectOffsetOnAxis(ap=sb_gidx[:, tt, k:k + 1], axis=0),
                         c_payload[:, tt, k, :], None), [(GP, 16)])
        tsc_done = GP.val

        _rel2 = ["hbuf", "htmp", "gw", "r32", "rz", "isl", "kcols",
                 "B0", "stats", "ropetmp", "zrow", "pm", "diag", "payload",
                 "eiota", "ltones", "ltstrict", "onesrow", "onescol",
                 "mself", "mselb", "jranks", "cvec"]
        if "ot" not in dbg_outs:
            _rel2.append("OT")
        if "w" not in dbg_outs:
            _rel2.append("wgt")
        A.release(*_rel2)
        phase_barrier()
        sb_w13 = A.alloc("w13s", (128, 2, 2, H), BF16)
        sb_w2 = A.alloc("w2s", (128, 1, 8, H), BF16)
        sb_btok = A.alloc("btok", (128, 2, 2, H), BF16)
        sb_bufT = A.alloc("bufT", (128, 2, 16, 256), BF16)
        sb_act = A.alloc("acts", (128, 2, 8, 256), BF16)
        sb_silu = A.alloc("silu", (128, 256), BF16)
        sb_y = A.alloc("ybuf", (128, 2, 2, H), BF16)
        sb_tg = A.alloc("tgq", (128, 2, 2, 2), I32)
        dbg_ge0_v = None
        if "ge0" in dbg_outs:
            dbg_ge0_v = A.alloc_r("dbgge0", (128, 22, 1024), BF16)
        es_free = [(GP, 0), (GP, 0)]
        tg_ld = [0, 0]
        w13_free = [(PE, 0), (PE, 0)]
        w13_ld = [0, 0]
        w2_free = [(PE, 0), (PE, 0)]
        w2_ld = [0, 0]
        for ee in range(E):
            es = ee % 2
            step("sync", [es_free[es], (GP, tsc_done)],
                 lambda e, ee=ee, es=es: e.dma_start(
                     out=sb_tg[:, es, 0, :],
                     in_=t_dram[ee * 257:ee * 257 + 128, :]), [(SL[2], 16)])
            step("sync", [],
                 lambda e, ee=ee, es=es: e.dma_start(
                     out=sb_tg[:, es, 1, :],
                     in_=t_dram[ee * 257 + 128:ee * 257 + 256, :]), [(SL[2], 16)])
            tg_ld[es] = SL[2].val
            for half in range(2):
                step("gpsimd", [(SL[2], tg_ld[es])],
                     lambda e, es=es, half=half: e.indirect_dma_start(
                         sb_btok[:, es, half, :], None,
                         h_dram,
                         IndirectOffsetOnAxis(ap=sb_tg[:, es, half, 0:1], axis=0)),
                     [(GP, 16)])
            g_ld = GP.val
            tpb_ = 7
            for half in range(2):
                for hb in range(16):
                    pv = pb[tpb_].bitcast(BF16)[:, (hb % 8) * 128:(hb % 8) * 128 + 128]
                    ev_s, ev_v = bank_evac[tpb_]
                    step("tensor", [(ev_s, ev_v), (GP, g_ld)],
                         lambda t, es=es, half=half, hb=hb, pv=pv: t.transpose(
                             pv, sb_btok[:, es, half, hb * 128:(hb + 1) * 128],
                             c_ident), [(PE, 1)])
                    bank_fill[tpb_] = PE.val
                    step("vector", [(PE, bank_fill[tpb_])],
                         lambda e, es=es, half=half, hb=hb, pv=pv: e.tensor_copy(
                             sb_bufT[:, es, hb, half * 128:(half + 1) * 128], pv))
                    bank_evac[tpb_] = (DV, DV.val)
            bufT_dv = DV.val
            for b_ in range(8):
                ev_s, ev_v = bank_evac[b_]
                step("tensor", [(ev_s, ev_v), (DV, bufT_dv)], lambda e: e.nop())
            for hb2 in range(8):
                ws2 = hb2 % 2
                step("gpsimd", [w13_free[ws2]],
                     lambda e, ee=ee, hb2=hb2, ws2=ws2: e.dma_start(
                         out=sb_w13[:, ws2, :, :],
                         in_=w13_in[ee, hb2 * 256:(hb2 + 1) * 256, :].rearrange(
                             "(a p) f -> p a f", a=2)), [(GP, 16)])
                w13_ld[ws2] = GP.val
                step("tensor", [(GP, w13_ld[ws2])], lambda e: e.nop())
                for hb in (hb2 * 2, hb2 * 2 + 1):
                    for p in range(8):
                        mm(pb[p][:, 0:256],
                           sb_w13[:, ws2, hb % 2, p * 128:(p + 1) * 128],
                           sb_bufT[:, es, hb, :], hb == 0, False)
                        mm(pb[p][:, 256:512],
                           sb_w13[:, ws2, hb % 2, (p + 8) * 128:(p + 9) * 128],
                           sb_bufT[:, es, hb, :], False, hb == 15,
                           inc=(p == 7))
                w13_free[ws2] = (PE, PE.val)
            gu_fill = PE.val
            for p in range(8):
                bank_fill[p] = gu_fill
                step("scalar", [(PE, gu_fill), (DV, None)],
                     lambda e, p=p: e.activation(
                         out=sb_silu, in_=pb[p][:, 0:256], func=AF.Silu))
                ac_si = AC.val
                step("vector", [(AC, ac_si)],
                     lambda e, es=es, p=p: e.tensor_mul(
                         sb_act[:, es, p, :], sb_silu, pb[p][:, 256:512]))
                bank_evac[p] = (DV, DV.val)
            act_dv = DV.val
            for i8 in range(8):
                step("gpsimd", [w2_free[0]] if i8 == 0 else [],
                     lambda e, ee=ee, i8=i8: e.dma_start(
                         out=sb_w2[:, 0, i8, :],
                         in_=w2_in[ee, i8 * 128:(i8 + 1) * 128, :]), [(GP, 16)])
            w2_ld[0] = GP.val
            yb = [6, 7]
            step("tensor", [(GP, w2_ld[0]), (DV, act_dv)], lambda e: e.nop())
            for ho in range(4):
                for ch in range(2):
                    ev_s, ev_v = bank_evac[yb[ch]]
                    step("tensor", [(ev_s, ev_v)], lambda e: e.nop())
                for i8 in range(8):
                    for ch in range(2):
                        mm(pb[yb[ch]][:, :],
                           sb_act[:, es, i8, ch * 128:(ch + 1) * 128],
                           sb_w2[:, 0, i8, ho * 512:(ho + 1) * 512],
                           i8 == 0, i8 == 7, inc=(ch == 1))
                bank_fill[yb[0]] = PE.val
                bank_fill[yb[1]] = PE.val
                for ch in range(2):
                    step("vector", [(PE, bank_fill[yb[ch]])],
                         lambda e, es=es, ch=ch, ho=ho: e.tensor_copy(
                             sb_y[:, es, ch, ho * 512:(ho + 1) * 512],
                             pb[yb[ch]][:, :]))
                    bank_evac[yb[ch]] = (DV, DV.val)
            y_dv = DV.val
            if "ge0" in dbg_outs and ee == 0:
                for hb in range(16):
                    step("vector", [], lambda e, hb=hb: e.tensor_copy(
                        dbg_ge0_v[:, hb, 0:256], sb_bufT[:, 0, hb, :]))
                for p8 in range(8):
                    step("vector", [], lambda e, p8=p8: e.tensor_copy(
                        dbg_ge0_v[:, 16 + p8 // 4, p8 % 4 * 256:(p8 % 4 + 1) * 256],
                        sb_act[:, 0, p8, :]))
                for ch in range(2):
                    step("vector", [], lambda e, ch=ch: e.tensor_copy(
                        dbg_ge0_v[:, 18 + ch * 2:20 + ch * 2, :].rearrange(
                            "p a b -> p (a b)"), sb_y[:, 0, ch, :]))
            for ch in range(2):
                step("gpsimd", [(DV, y_dv)],
                     lambda e, es=es, ch=ch: e.indirect_dma_start(
                         s_dram,
                         IndirectOffsetOnAxis(ap=sb_tg[:, es, ch, 1:2], axis=0),
                         sb_y[:, es, ch, :], None), [(GP, 16)])
            es_free[es] = (GP, GP.val)
            w2_free[0] = (PE, PE.val)
        experts_done_gp = GP.val

        A.release("w13s", "w2s", "btok", "bufT", "acts", "silu", "ybuf", "tgq")
        phase_barrier()
        sb_sg = A.alloc("sgs", (128, 2, 16, 128), BF16)
        sb_su = A.alloc("sus", (128, 2, 16, 128), BF16)
        sb_ash = A.alloc("ash", (128, 8, TCORE), BF16)
        sb_sil2 = A.alloc("sil2", (128, 512), BF16)
        sgu_free = [(PE, 0), (PE, 0)]
        sgu_ld = [0, 0]
        for i8 in range(8):
            ws = i8 % 2
            step("gpsimd", [sgu_free[ws]], lambda e: e.nop())
            for hb in range(16):
                step("gpsimd", [],
                     lambda e, i8=i8, hb=hb, ws=ws: e.dma_start(
                         out=sb_sg[:, ws, hb, :],
                         in_=sg_in[hb * 128:(hb + 1) * 128,
                                   i8 * 128:(i8 + 1) * 128]), [(GP, 16)])
                step("gpsimd", [],
                     lambda e, i8=i8, hb=hb, ws=ws: e.dma_start(
                         out=sb_su[:, ws, hb, :],
                         in_=su_in[hb * 128:(hb + 1) * 128,
                                   i8 * 128:(i8 + 1) * 128]), [(GP, 16)])
            sgu_ld[ws] = GP.val
            for tc in range(2):
                gb, ub = tc, 2 + tc
                for b_ in (gb, ub):
                    ev_s, ev_v = bank_evac[b_]
                    step("tensor", [(ev_s, ev_v), (GP, sgu_ld[ws])],
                         lambda e: e.nop())
                for hb in range(16):
                    mm(pb[gb][:, :], sb_sg[:, ws, hb, :],
                       sb_hT[:, hb, tc * 512:(tc + 1) * 512],
                       hb == 0, hb == 15)
                    mm(pb[ub][:, :], sb_su[:, ws, hb, :],
                       sb_hT[:, hb, tc * 512:(tc + 1) * 512],
                       hb == 0, hb == 15, inc=(hb == 15))
                bank_fill[gb] = PE.val
                bank_fill[ub] = PE.val
                step("scalar", [(PE, bank_fill[ub]), (DV, None)],
                     lambda e, gb=gb: e.activation(
                         out=sb_sil2, in_=pb[gb][:, :], func=AF.Silu))
                ac_s2 = AC.val
                step("vector", [(AC, ac_s2)],
                     lambda e, i8=i8, tc=tc, ub=ub: e.tensor_mul(
                         sb_ash[:, i8, tc * 512:(tc + 1) * 512],
                         sb_sil2, pb[ub][:, :]))
                bank_evac[gb] = (DV, DV.val)
                bank_evac[ub] = (DV, DV.val)
            sgu_free[ws] = (PE, PE.val)
        ash_dv = DV.val
        A.release("sgs", "sus", "sil2", "hT")
        phase_barrier()

        sb_sd = A.alloc("sds", (128, 8, H), BF16)
        sb_srb = A.alloc("srb", (128, 2, 4, H), BF16)
        sb_base = A.alloc("base", (128, 2, H), F32)
        NYO = 1 if "ge0" in dbg_outs else 2
        sb_yout = A.alloc("yo", (128, NYO, H), F32)
        sb_nmr = A.alloc("nmr", (128, 2), F32)
        for i8 in range(8):
            step("gpsimd", [], lambda e, i8=i8: e.dma_start(
                out=sb_sd[:, i8, :], in_=sd_in[i8 * 128:(i8 + 1) * 128, :]),
                 [(GP, 16)])
        sd_ld = GP.val
        srb_free = [(DV, 0), (DV, 0)]
        yout_free = [(STo, 0), (STo, 0)]
        yosl = (lambda slot: slot % NYO)
        for tt in range(NTT):
            slot = tt % 2
            step("sync", [srb_free[slot], (GP, experts_done_gp)],
                 lambda e, tt=tt, slot=slot: e.dma_start(
                     out=sb_srb[:, slot, :, :],
                     in_=s_dram[1 + tt * 512:1 + (tt + 1) * 512, :].rearrange(
                         "(p a) h -> p a h", a=4)), [(SL[3], 16)])
            srb_ld = SL[3].val
            m2 = sb_s2[:, tt, 0:1]
            r1 = sb_s2[:, tt, 1:2]
            nmr = sb_nmr[:, slot:slot + 1]
            step("vector", [], lambda e, m2=m2, r1=r1, nmr=nmr: e.tensor_mul(
                nmr, m2, r1))
            step("vector", [], lambda e, nmr=nmr: e.tensor_scalar(
                nmr, nmr, -1.0, None, OP.mult))
            nmr_dv = DV.val
            step("scalar", [(DV, nmr_dv)],
                 lambda e, tt=tt, slot=slot, r1=r1, nmr=nmr: e.activation(
                     out=sb_base[:, slot, :], in_=sb_x1[:, tt, :],
                     func=AF.Identity, scale=r1, bias=nmr))
            ac_b = AC.val
            step("vector", [(AC, ac_b)],
                 lambda e, tt=tt, slot=slot: e.tensor_add(
                     sb_base[:, slot, :], sb_base[:, slot, :], sb_x1[:, tt, :]))
            step("vector", [(SL[3], srb_ld)], lambda e: e.nop())
            for k in range(TOPK):
                step("vector", [],
                     lambda e, tt=tt, slot=slot, k=k: e.scalar_tensor_tensor(
                         out=sb_base[:, slot, :], in0=sb_srb[:, slot, k, :],
                         scalar=sb_wt[:, tt, k:k + 1], in1=sb_base[:, slot, :],
                         op0=OP.mult, op1=OP.add))
            base_dv = DV.val
            srb_free[slot] = (DV, base_dv)
            for ho in range(4):
                bank = 4 + ho % 2
                ev_s, ev_v = bank_evac[bank]
                step("tensor", [(ev_s, ev_v), (GP, sd_ld), (DV, ash_dv)],
                     lambda e: e.nop())
                for i8 in range(8):
                    mm(pb[bank][:, :], sb_ash[:, i8, tt * 128:(tt + 1) * 128],
                       sb_sd[:, i8, ho * 512:(ho + 1) * 512],
                       i8 == 0, i8 == 7, inc=(i8 == 7))
                bank_fill[bank] = PE.val
                step("vector", [(PE, bank_fill[bank]), (DV, base_dv),
                                yout_free[yosl(slot)]],
                     lambda e, tt=tt, slot=slot, ho=ho, bank=bank:
                     e.scalar_tensor_tensor(
                         out=sb_yout[:, yosl(slot), ho * 512:(ho + 1) * 512],
                         in0=pb[bank][:, :], scalar=1.0,
                         in1=sb_base[:, slot, ho * 512:(ho + 1) * 512],
                         op0=OP.mult, op1=OP.add))
                bank_evac[bank] = (DV, DV.val)
            yo_dv = DV.val
            step("sync", [(DV, yo_dv)],
                 lambda e, tt=tt, slot=slot: e.dma_start(
                     out=y_out[tt * 128:(tt + 1) * 128, :],
                     in_=sb_yout[:, yosl(slot), :]), [(STo, 16)])
            yout_free[yosl(slot)] = (STo, STo.val)

        def dump(name, src_ap):
            if name not in dbg_outs:
                return
            a = src_ap.shape[1] if len(src_ap.shape) > 2 else 1
            b = src_ap.shape[-1]
            tmp = A.alloc("dbgtmp", (src_ap.shape[0], b), F32)
            for i in range(a):
                sv = src_ap[:, i, :] if a > 1 else src_ap
                dv_ = dbg_outs[name][:, i, :] if a > 1 else dbg_outs[name]
                step("vector", [(DV, None), (PE, None), (AC, None), (STo, None)],
                     lambda e, sv=sv: e.tensor_copy(tmp, sv), [(DV, 1)])
                step("sync", [(DV, None)],
                     lambda e, dv_=dv_: e.dma_start(out=dv_, in_=tmp), [(STo, 16)])
            step("vector", [(STo, None)], lambda e: e.nop())
            A.release("dbgtmp")

        if "lnx0" in dbg_outs:
            step("sync", [(DV, None)],
                 lambda e: e.dma_start(out=dbg_outs["lnx0"],
                                       in_=A.view("dbglnx")), [(STo, 16)])
        if "p00" in dbg_outs:
            for i in range(16):
                step("sync", [(DV, None), (AC, None), (PE, None)],
                     lambda e, i=i: e.dma_start(out=dbg_outs["p00"][:, i, :],
                                                in_=A.view("dbgp")[:, i, :]),
                     [(STo, 16)])

        dump("kt", sb_KT)
        dump("qt", sb_QT)
        dump("v", sb_V)
        dump("ot", sb_OT)
        dump("bt", sb_BT)
        dump("w", sb_w)
        if "ge0" in dbg_outs:
            for i in range(22):
                step("gpsimd", [(DV, None), (PE, None)],
                     lambda e, i=i: e.dma_start(out=dbg_outs["ge0"][:, i, :],
                                                in_=dbg_ge0_v[:, i, :]),
                     [(GP, 16)])
        if "islot" in dbg_outs:
            for i in range(NTT):
                step("sync", [(DV, None), (PE, None)],
                     lambda e, i=i, dv_=dbgisl_v: e.dma_start(
                         out=dbg_outs["islot"][:, i, :],
                         in_=dv_[:, i, :]), [(STo, 16)])
        if "x1" in dbg_outs:
            tmp = A.alloc("dbgtmp", (128, H), F32)
            for tt in range(NTT):
                step("vector", [(DV, None), (PE, None), (AC, None), (STo, None)],
                     lambda e, tt=tt: e.tensor_copy(tmp, sb_x1[:, tt, :]),
                     [(DV, 1)])
                step("sync", [(DV, None)],
                     lambda e, tt=tt: e.dma_start(
                         out=dbg_outs["x1"][tt * 128:(tt + 1) * 128, :], in_=tmp),
                     [(STo, 16)])
            step("vector", [(STo, None)], lambda e: e.nop())
            A.release("dbgtmp")

        final_st = STo.val
        step("sync", [(STo, final_st), (GP, None), (LD, None), (DV, None),
                      (PE, None), (AC, None)], lambda e: e.nop())

        @block.sync
        def _(sync):
            for f in Q["sync"]:
                f(sync)

        @block.tensor
        def _(tensor):
            for f in Q["tensor"]:
                f(tensor)

        @block.vector
        def _(vector):
            for f in Q["vector"]:
                f(vector)

        @block.scalar
        def _(scalar):
            for f in Q["scalar"]:
                f(scalar)

        @block.gpsimd
        def _(gpsimd):
            for f in Q["gpsimd"]:
                f(gpsimd)

    return nc


def shard_inputs(inputs):
    x = np.ascontiguousarray(inputs["L0_attn_hidden_states"])
    hc = _host_constants()
    per_core = []
    shared = dict(
        wq=inputs["L0_attn_wq"], wk=inputs["L0_attn_wk"],
        wv=inputs["L0_attn_wv"], wo=inputs["L0_attn_wo"],
        norm_w=inputs["L0_ffn_norm_w"],
        gwT=np.ascontiguousarray(inputs["L0_ffn_gate_w"].T),
        w13=inputs["L0_ffn_w13"], w2=inputs["L0_ffn_w2"],
        sgw=inputs["L0_ffn_shared_gate_w"], suw=inputs["L0_ffn_shared_up_w"],
        sdw=inputs["L0_ffn_shared_down_w"],
        ident_bf=hc["ident_bf"], ones_row=hc["ones_row"], ones_col=hc["ones_col"],
        lt_ones=hc["lt_ones"], lt_strict=hc["lt_strict"], e_iota=hc["e_iota"],
        payload=hc["payload"], t_prefill=hc["t_prefill"],
        cvec=np.broadcast_to(np.array(
            [1.0 / H, -1.0, EPS1, -EPS1, 1.0 + EPS2, 0.0, 0.0, 0.0],
            np.float32), (128, 8)).copy(),
    )
    shared = {k: np.ascontiguousarray(v) for k, v in shared.items()}
    for core in range(8):
        b, g = core // 2, core % 2
        cs = CHUNKSETS[g]
        cc = _core_constants(g)
        rows = np.concatenate([np.arange(c * CHUNK, (c + 1) * CHUNK) for c in cs])
        m = dict(shared)
        m["x_own"] = np.ascontiguousarray(x[b][rows])
        m["x_ctx"] = np.ascontiguousarray(x[b])
        for k in ("cosq", "sinq", "cosk", "sink", "ctile", "pm", "diag"):
            m[k] = np.ascontiguousarray(cc[k])
        per_core.append(m)
    return per_core


def unshard_output(results):
    out = np.zeros((B, S, H), np.float32)
    for core in range(8):
        b, g = core // 2, core % 2
        cs = CHUNKSETS[g]
        y = results[core]["y"]
        for i, c in enumerate(cs):
            out[b, c * CHUNK:(c + 1) * CHUNK, :] = y[i * CHUNK:(i + 1) * CHUNK]
    return out


def kernel(**inputs):
    nc = build_program(dbg=_DBG)
    in_maps = shard_inputs(inputs)
    res = run_bass_kernel_spmd(nc, in_maps, core_ids=list(range(8)))
    return unshard_output(res.results)
